# version 02 (final)
# baseline (speedup 1.0000x reference)
import numpy as np
import ml_dtypes
from contextlib import ExitStack

import concourse.bass as bass
import concourse.mybir as mybir
from concourse.bass_utils import run_bass_kernel_spmd

F32 = mybir.dt.float32
BF16 = mybir.dt.bfloat16
AF = mybir.ActivationFunctionType
ALU = mybir.AluOpType
AX = mybir.AxisListType
BF = ml_dtypes.bfloat16

NB = 4
SQ = 4096
DM = 1024
TL = 2048
DCH = 8
TC = TL // 128
TB = TL // 512
MAGIC = 12582912.0
GROUPS = [[0, 1], [2, 3], [4, 5], [6, 7]]
LN_EPS = 1e-5
RMS_EPS = 1e-6

PAIRSWAP = []
for _i in range(16):
    PAIRSWAP += [2 * _i + 1, 2 * _i]

C_MAGIC, C_LNEPS, C_RMSEPS, C_INV1024, C_WSEPS, C_ONE, C_INVHD, C_INV4M = range(8)
DEBUG_ATTN = False


class Prog:
    def __init__(self, nc):
        self.nc = nc
        self.es = ExitStack()
        self.sem = {}
        self.cnt = {}

    def S(self, name):
        if name not in self.sem:
            self.sem[name] = self.es.enter_context(self.nc.semaphore(name))
            self.cnt[name] = 0
        return self.sem[name]

    def base(self, name):
        self.S(name)
        return self.cnt[name]

    def inc(self, ins, name, k=1):
        ins.then_inc(self.S(name), k)
        self.cnt[name] += k
        return self.cnt[name]

    def wait(self, eng, name, at=None):
        eng.wait_ge(self.S(name), self.cnt[name] if at is None else at)

    def dma(self, eng, out, in_, q="sd"):
        return self.inc(eng.dma_start(out=out, in_=in_), q, 16)

    def sb(self, name, shape, dtype):
        return self.es.enter_context(self.nc.sbuf_tensor(name, shape, dtype))


def pipe(p, tag, n, stages, nbuf=2):
    nc = p.nc
    names = [f"{tag}{s}" for s in range(len(stages))]
    bases = [p.base(nm) for nm in names]
    amts = [st[2] for st in stages]
    last = len(stages) - 1
    with nc.Block() as blk:
        for s, (engname, fn, amt) in enumerate(stages):
            def section(e, s=s, fn=fn, amt=amt):
                for i in range(n):
                    if s > 0:
                        e.wait_ge(p.S(names[s - 1]), bases[s - 1] + (i + 1) * amts[s - 1])
                    if s < last and i >= nbuf:
                        e.wait_ge(p.S(names[s + 1]), bases[s + 1] + (i - nbuf + 1) * amts[s + 1])
                    if s == 0 and i >= nbuf:
                        e.wait_ge(p.S(names[-1]), bases[-1] + (i - nbuf + 1) * amts[-1])
                    ins = fn(e, i, i % nbuf)
                    p.inc(ins, names[s], amt)
            getattr(blk, engname)(section)
    return names



def xt3(XT):
    return XT[:, :].rearrange("p (c n) -> p c n", c=DCH)


def fv(t, off, n):
    return t[:, off:off + n]


def fv3(t, off, a, b):
    return t[:, off:off + a * b].rearrange("p (a b) -> p a b", a=a)


def bv(t, off, n):
    return t.ap().bitcast(BF16)[:, 2 * off:2 * off + n]


def bv3(t, off, a, b):
    return t.ap().bitcast(BF16)[:, 2 * off:2 * off + a * b].rearrange("p (a b) -> p a b", a=a)



def emit_setup(p, cdr):
    nc = p.nc
    XT = p.sb("XT", [128, DCH * TL], F32)
    A = p.sb("arA", [128, 8192], F32)
    B = p.sb("arB", [128, 8192], F32)
    C = p.sb("arC", [128, 8192], F32)
    D = p.sb("arD", [128, 4096], F32)
    SM = p.sb("arSM", [128, 2048], F32)
    mbc = p.sb("mbc", [128, TL], F32)
    rbc = p.sb("rbc", [128, TL], F32)
    idb = p.sb("idb", [128, 128], BF16)
    idf = p.sb("idf", [128, 128], F32)
    ones1 = p.sb("ones1", [1, 128], F32)
    onesc = p.sb("onesc", [128, 1], F32)
    cst = p.sb("cst", [128, 8], F32)
    x_in = nc.dram_tensor("xT", [DM, TL], F32, kind="ExternalInput")
    x_park = nc.dram_tensor("x_park", [DM, TL], F32, kind="Internal")
    with nc.Block() as blk:
        @blk.sync
        def _(e):
            p.dma(e, xt3(XT), x_in.ap().rearrange("(c p) n -> p c n", p=128))
            p.dma(e, idb[:, :], cdr["idb"][:, :])
            p.dma(e, idf[:, :], cdr["idf"][:, :])
            p.dma(e, ones1[:, :], cdr["ones1"][:, :])
            p.dma(e, onesc[:, :], cdr["onesc"][:, :])
            p.dma(e, cst[:, :], cdr["cst"][:, :])
            p.wait(e, "sd")
    return dict(XT=XT, A=A, B=B, C=C, D=D, SM=SM, mbc=mbc, rbc=rbc, idb=idb, idf=idf,
                ones1=ones1, onesc=onesc, cst=cst, cdr=cdr, x_park=x_park)


def emit_settle(p, env, blk):
    @blk.vector
    def _(v):
        p.inc(v.tensor_copy(env["SM"][0:1, 0:1], env["cst"][0:1, 0:1]), "settle", 1)


def emit_store_x(p, env, x_out):
    with p.nc.Block() as blk:
        emit_settle(p, env, blk)
        @blk.sync
        def _(e):
            p.wait(e, "settle")
            p.dma(e, x_out.ap().rearrange("(c p) n -> p c n", p=128), xt3(env["XT"]))
            p.wait(e, "sd")
            p.wait(e, "gd")


def emit_park_x(p, env):
    with p.nc.Block() as blk:
        emit_settle(p, env, blk)
        @blk.sync
        def _(e):
            p.wait(e, "settle")
            p.dma(e, env["x_park"].ap().rearrange("(c p) n -> p c n", p=128), xt3(env["XT"]))
            p.wait(e, "sd")


def emit_unpark_x(p, env):
    with p.nc.Block() as blk:
        @blk.sync
        def _(e):
            p.dma(e, xt3(env["XT"]), env["x_park"].ap().rearrange("(c p) n -> p c n", p=128))
            p.wait(e, "sd")



def emit_ln_rows(p, env, tag, src_tile_fn, n_iter, iter_cd, rms, eps_col, sums, sqs, var,
                 sqb=None, dma_stage=None):
    nc = p.nc
    with ExitStack() as es:
        pss = es.enter_context(nc.psum_tensor(f"{tag}_pss", [1, 512], F32))
        psq = es.enter_context(nc.psum_tensor(f"{tag}_psq", [1, 512], F32))
        b_q = p.base(f"{tag}q")
        b_m = p.base(f"{tag}m")
        b_e = p.base(f"{tag}e")
        b_d = p.base("sd")
        with nc.Block() as blk:
            if dma_stage is not None:
                @blk.sync
                def _(e):
                    for k in range(n_iter):
                        if k >= 2:
                            e.wait_ge(p.S(f"{tag}q"), b_q + k - 1)
                            e.wait_ge(p.S(f"{tag}m"), b_m + k - 1)
                        p.inc(dma_stage(e, k, k % 2), "sd", 16)
            @blk.scalar
            def _(a):
                for k in range(n_iter):
                    tb, cd = iter_cd(k)
                    if dma_stage is not None:
                        a.wait_ge(p.S("sd"), b_d + (k + 1) * 16)
                        src = src_tile_fn(k, k % 2)
                    else:
                        src = src_tile_fn(k, None)
                    if k >= 2:
                        a.wait_ge(p.S(f"{tag}m"), b_m + k - 1)
                    p.inc(a.square(sqb[:, k % 2, :], src), f"{tag}q", 1)
            @blk.tensor
            def _(t):
                for k in range(n_iter):
                    tb, cd = iter_cd(k)
                    if cd == 0 and tb > 0:
                        t.wait_ge(p.S(f"{tag}e"), b_e + tb)
                    if not rms:
                        if dma_stage is not None:
                            t.wait_ge(p.S("sd"), b_d + (k + 1) * 16)
                            src = src_tile_fn(k, k % 2)
                        else:
                            src = src_tile_fn(k, None)
                        t.matmul(pss[:, :], env["onesc"][:, :], src,
                                 start=(cd == 0), stop=(cd == DCH - 1))
                    t.wait_ge(p.S(f"{tag}q"), b_q + k + 1)
                    ins = t.matmul(psq[:, :], env["onesc"][:, :], sqb[:, k % 2, :],
                                   start=(cd == 0), stop=(cd == DCH - 1))
                    p.inc(ins, f"{tag}m", 1)
            @blk.vector
            def _(v):
                for tb in range(n_iter // DCH):
                    v.wait_ge(p.S(f"{tag}m"), b_m + (tb + 1) * DCH)
                    if not rms:
                        v.tensor_copy(sums[:, tb * 512:(tb + 1) * 512], pss[:, :])
                    p.inc(v.tensor_copy(sqs[:, tb * 512:(tb + 1) * 512], psq[:, :]),
                          f"{tag}e", 1)
        with nc.Block() as blk:
            @blk.vector
            def _(v):
                if rms:
                    v.tensor_scalar(var[:, :], sqs[:, :], 1.0 / DM, None, op0=ALU.mult)
                else:
                    v.tensor_scalar(sums[:, :], sums[:, :], 1.0 / DM, None, op0=ALU.mult)
                    v.tensor_scalar(var[:, :], sqs[:, :], 1.0 / DM, None, op0=ALU.mult)
                    v.tensor_tensor(sqs[:, :], sums[:, :], sums[:, :], op=ALU.mult)
                    v.tensor_tensor(var[:, :], var[:, :], sqs[:, :], op=ALU.subtract)
            @blk.scalar
            def _(a):
                pass
        with nc.Block() as blk:
            @blk.scalar
            def _(a):
                a.activation(var[:, :], var[:, :], AF.Sqrt,
                             bias=env["cst"][0:1, eps_col:eps_col + 1], scale=1.0)
            @blk.vector
            def _(v):
                pass
        with nc.Block() as blk:
            @blk.vector
            def _(v):
                v.reciprocal(var[:, :], var[:, :])
            @blk.scalar
            def _(a):
                pass


def emit_bc(p, env, row, bc, tag):
    nc = p.nc
    with ExitStack() as es:
        psb = es.enter_context(nc.psum_tensor(f"{tag}_psb", [128, 2, 512], F32))
        pipe(p, tag, TB, [
            ("tensor", lambda t, i, b: t.matmul(psb[:, b, :], env["ones1"][:, :],
                                                row[:, i * 512:(i + 1) * 512],
                                                start=True, stop=True), 1),
            ("vector", lambda v, i, b: v.tensor_copy(bc[:, i * 512:(i + 1) * 512],
                                                     psb[:, b, :]), 1),
        ])


def ln_scratch(W):
    return dict(sums=W[0:1, 0:TL], sqs=W[0:1, TL:2 * TL], var=W[0:1, 2 * TL:3 * TL],
                sqb=fv3(W, 3 * TL, 2, 512))


def emit_ln_full(p, env, src3, rms, eps_col, tag, W):
    sc = ln_scratch(W)
    emit_ln_rows(p, env, tag, lambda k, b: src3[:, k % DCH, (k // DCH) * 512:(k // DCH + 1) * 512],
                 TB * DCH, lambda k: (k // DCH, k % DCH), rms, eps_col,
                 sc["sums"], sc["sqs"], sc["var"], sqb=sc["sqb"])
    if not rms:
        emit_bc(p, env, sc["sums"], env["mbc"], tag + "m")
    emit_bc(p, env, sc["var"], env["rbc"], tag + "r")


def emit_apply_ln(p, env, src3, dst3, tmp, gain_pc=None, use_mean=True):
    nc = p.nc
    if True:
        with nc.Block() as blk:
            @blk.vector
            def _(v):
                for c in range(DCH):
                    for tb in range(TB):
                        sl = slice(tb * 512, (tb + 1) * 512)
                        if use_mean:
                            v.tensor_tensor(tmp[:, :], src3[:, c, sl], env["mbc"][:, sl],
                                            op=ALU.subtract)
                            s = tmp[:, :]
                        else:
                            s = src3[:, c, sl]
                        if gain_pc is None:
                            v.tensor_tensor(dst3[:, c, sl], s, env["rbc"][:, sl], op=ALU.mult)
                        else:
                            v.tensor_tensor(tmp[:, :], s, env["rbc"][:, sl], op=ALU.mult)
                            v.tensor_scalar(dst3[:, c, sl], tmp[:, :], gain_pc[:, c:c + 1],
                                            None, op0=ALU.mult)
            @blk.scalar
            def _(a):
                pass


def emit_w_load(p, WB3, w_dram, nchunk=DCH, cols=DM, sem="gd"):
    with p.nc.Block() as blk:
        @blk.gpsimd
        def _(g):
            p.dma(g, WB3[:, 0:nchunk, 0:cols],
                  w_dram.ap().rearrange("(c p) n -> p c n", p=128), q=sem)
            p.wait(g, sem)


def emit_collective(p, kind, op, src_ap, dst_ap):
    with p.nc.Block() as blk:
        @blk.gpsimd
        def _(g):
            p.wait(g, "gd")
            p.wait(g, "sd")
            p.inc(g.collective_compute(kind, op, replica_groups=GROUPS,
                                       ins=[src_ap], outs=[dst_ap]), "cc", 1)
            p.wait(g, "cc")



def emit_l0_attn(p, env):
    nc = p.nc
    XT3 = xt3(env["XT"])
    A, B, C, D, SM = env["A"], env["B"], env["C"], env["D"], env["SM"]
    Wq = nc.dram_tensor("l0a_wq", [DM, DM], F32, kind="ExternalInput")
    Wk = nc.dram_tensor("l0a_wk", [DM, DM], F32, kind="ExternalInput")
    Wv = nc.dram_tensor("l0a_wv", [DM, DM], F32, kind="ExternalInput")
    Wo = nc.dram_tensor("l0a_wo", [DM, DM], F32, kind="ExternalInput")
    vecs = nc.dram_tensor("l0a_vecs", [128, 24], F32, kind="ExternalInput")
    k_scr = nc.dram_tensor("l0a_kscr", [DM, TL], F32, kind="Internal")
    s_scr = nc.dram_tensor("l0a_sscr", [DM, TL], F32, kind="Internal")
    ctx_loc = nc.dram_tensor("l0a_ctxl", [DM, DM], F32, kind="Internal")
    ctx_red = nc.dram_tensor("l0a_ctxr", [DM, DM], F32, kind="Internal")
    stm_l = nc.dram_tensor("l0a_stml", [128, 8], F32, kind="Internal")
    stm_r = nc.dram_tensor("l0a_stmr", [128, 8], F32, kind="Internal")
    sts_l = nc.dram_tensor("l0a_stsl", [128, 8], F32, kind="Internal")
    sts_r = nc.dram_tensor("l0a_stsr", [128, 8], F32, kind="Internal")

    H0 = bv3(A, 0, DCH, TL)
    KET = bv3(A, 0, TC, DM)
    ATT = bv3(A, 0, DCH, TL)
    QSMT = bv3(B, 0, DCH, TL)
    QSM = bv3(C, 0, TC, DM)
    VT = bv3(C, 0, TC, DM)
    WO2 = bv3(C, 0, DCH, DM)
    WB = bv3(D, 0, DCH, DM)
    KFr = fv3(D, 0, 2, TL)
    CTXB = bv3(D, 0, DCH, DM)
    KEr = bv3(SM, 0, 2, TL)
    CSTG = fv3(SM, 0, 2, 512)
    CFR = fv3(SM, 0, 2, DM)
    SSTG = fv3(SM, 0, 2, 512)
    HREC = fv3(SM, 1024, 2, 512)
    SRLD = fv3(SM, 0, 2, 512)

    with ExitStack() as es:
        vpc = es.enter_context(nc.sbuf_tensor("l0a_vpc", [128, 24], F32))
        kst = es.enter_context(nc.sbuf_tensor("l0a_kst", [128, 8], F32))
        rks = es.enter_context(nc.sbuf_tensor("l0a_rks", [128, 8], F32))
        kmx = es.enter_context(nc.sbuf_tensor("l0a_kmx", [128, 8], F32))
        ksum = es.enter_context(nc.sbuf_tensor("l0a_ksum", [128, 8], F32))
        kbm = es.enter_context(nc.sbuf_tensor("l0a_kbm", [128, DCH, TB], F32))
        with nc.Block() as blk:
            @blk.sync
            def _(e):
                p.dma(e, vpc[:, :], vecs[:, :])
                p.wait(e, "sd")
        bo_pc = vpc[:, 0:8]
        ga_pc = vpc[:, 8:16]
        be_pc = vpc[:, 16:24]

        emit_ln_full(p, env, XT3, rms=False, eps_col=C_LNEPS, tag="ln0", W=B)
        emit_apply_ln(p, env, XT3, H0, tmp=fv(SM, 0, 512))

        emit_w_load(p, WB, Wq)
        with ExitStack() as es2:
            psq = es2.enter_context(nc.psum_tensor("l0a_psq", [128, 2, 1024], F32))
            nmx = es2.enter_context(nc.sbuf_tensor("l0a_nmx", [128, 2], F32))
            qsum = es2.enter_context(nc.sbuf_tensor("l0a_qsum", [128, 16], F32))
            qrcp = es2.enter_context(nc.sbuf_tensor("l0a_qrcp", [128, 16], F32))

            def qp_mm(t, i, b):
                for nb in range(2):
                    for c in range(DCH):
                        ins = t.matmul(psq[:, b, nb * 512:(nb + 1) * 512],
                                       H0[:, c, i * 128:(i + 1) * 128],
                                       WB[:, c, nb * 512:(nb + 1) * 512],
                                       start=(c == 0), stop=(c == DCH - 1))
                return ins

            pipe(p, "l0aq", TC, [
                ("tensor", qp_mm, 1),
                ("vector", lambda v, i, b: v.reduce_max(nmx[:, b:b + 1], psq[:, b, :],
                                                        axis=AX.X, negate=True), 1),
                ("scalar", lambda a, i, b: a.activation(QSM[:, i, :], psq[:, b, :], AF.Exp,
                                                        bias=nmx[:, b:b + 1], scale=1.0,
                                                        accum_out=qsum[:, i:i + 1]), 1),
            ], nbuf=2)
            with nc.Block() as blk:
                @blk.vector
                def _(v):
                    p.inc(v.reciprocal(qrcp[:, :], qsum[:, :]), "l0aqr", 1)
                @blk.gpsimd
                def _(g):
                    p.wait(g, "l0aqr")
                    for i in range(TC):
                        g.tensor_scalar(QSM[:, i, :], QSM[:, i, :], qrcp[:, i:i + 1],
                                        0.03125, op0=ALU.mult, op1=ALU.mult)
            pst = es2.enter_context(nc.psum_tensor("l0a_pst", [128, 2, 1024], BF16))

            def qt_tr(t, i, b):
                for c in range(DCH):
                    ins = t.transpose(pst[:, b, c * 128:(c + 1) * 128],
                                      QSM[:, i, c * 128:(c + 1) * 128], env["idb"][:, :])
                return ins

            pipe(p, "l0aqt", TC, [
                ("tensor", qt_tr, 1),
                ("vector", lambda v, i, b: v.tensor_copy(
                    QSMT[:, :, i * 128:(i + 1) * 128],
                    pst[:, b, :].rearrange("p (c n) -> p c n", c=DCH)), 1),
            ], nbuf=2)

        emit_w_load(p, WB, Wv)
        with ExitStack() as es2:
            psv = es2.enter_context(nc.psum_tensor("l0a_psv", [128, 2, 1024], F32))

            def vp_mm(t, i, b):
                for nb in range(2):
                    for c in range(DCH):
                        ins = t.matmul(psv[:, b, nb * 512:(nb + 1) * 512],
                                       H0[:, c, i * 128:(i + 1) * 128],
                                       WB[:, c, nb * 512:(nb + 1) * 512],
                                       start=(c == 0), stop=(c == DCH - 1))
                return ins

            pipe(p, "l0av", TC, [
                ("tensor", vp_mm, 1),
                ("vector", lambda v, i, b: v.tensor_copy(VT[:, i, :], psv[:, b, :]), 1),
            ], nbuf=2)

        emit_w_load(p, WB, Wk)
        with ExitStack() as es2:
            KF1 = es2.enter_context(nc.sbuf_tensor("l0a_kf1", [128, 2, 512], F32))
            psk = es2.enter_context(nc.psum_tensor("l0a_psk", [128, 2, 512], F32))

            def k1_mm(t, i, b):
                co, tb = divmod(i, TB)
                for c in range(DCH):
                    ins = t.matmul(psk[:, b, :],
                                   WB[:, c, co * 128:(co + 1) * 128],
                                   H0[:, c, tb * 512:(tb + 1) * 512],
                                   start=(c == 0), stop=(c == DCH - 1))
                return ins

            def k1_ev(v, i, b):
                co, tb = divmod(i, TB)
                return v.tensor_scalar(KF1[:, b, :], psk[:, b, :], 0.0, None,
                                       op0=ALU.add, op1=ALU.max,
                                       accum_out=kbm[:, co, tb:tb + 1])

            def k1_st(e, i, b):
                co, tb = divmod(i, TB)
                return e.dma_start(out=k_scr.ap()[co * 128:(co + 1) * 128,
                                                  tb * 512:(tb + 1) * 512],
                                   in_=KF1[:, b, :])

            pipe(p, "l0ak1", DCH * TB, [
                ("tensor", k1_mm, 1),
                ("vector", k1_ev, 1),
                ("sync", k1_st, 16),
            ], nbuf=2)
        with nc.Block() as blk:
            @blk.vector
            def _(v):
                for co in range(DCH):
                    v.reduce_max(kmx[:, co:co + 1], kbm[:, co, :], axis=AX.X)
            @blk.scalar
            def _(a):
                pass

        with nc.Block() as blk:
            @blk.sync
            def _(e):
                p.dma(e, stm_l.ap()[:, :], kmx[:, :])
        emit_collective(p, "AllReduce", ALU.max, stm_l.ap().opt(), stm_r.ap().opt())
        with nc.Block() as blk:
            @blk.sync
            def _(e):
                p.dma(e, kst[:, :], stm_r.ap()[:, :])
        with nc.Block() as blk:
            @blk.vector
            def _(v):
                p.wait(v, "sd")
                v.tensor_scalar(kst[:, :], kst[:, :], -1.0, None, op0=ALU.mult)
            @blk.sync
            def _(e):
                pass

        with ExitStack() as es2:
            pst2 = es2.enter_context(nc.psum_tensor("l0a_pst2", [128, 2, 2048], BF16))

            def k2_ld(e, i, b):
                return e.dma_start(out=KFr[:, b, :], in_=k_scr.ap()[i * 128:(i + 1) * 128, :])

            def k2_exp(a, i, b):
                return a.activation(KEr[:, b, :], KFr[:, b, :], AF.Exp,
                                    bias=kst[:, i:i + 1], scale=1.0,
                                    accum_out=ksum[:, i:i + 1])

            def k2_tr(t, i, b):
                for tc in range(TC):
                    ins = t.transpose(pst2[:, b, tc * 128:(tc + 1) * 128],
                                      KEr[:, b, tc * 128:(tc + 1) * 128], env["idb"][:, :])
                return ins

            def k2_ev(v, i, b):
                return v.tensor_copy(
                    KET[:, :, i * 128:(i + 1) * 128],
                    pst2[:, b, :].rearrange("p (c n) -> p c n", c=TC))

            pipe(p, "l0ak2", DCH, [
                ("sync", k2_ld, 16),
                ("scalar", k2_exp, 1),
                ("tensor", k2_tr, 1),
                ("vector", k2_ev, 1),
            ], nbuf=2)

        with nc.Block() as blk:
            @blk.sync
            def _(e):
                p.dma(e, sts_l.ap()[:, :], ksum[:, :])
        emit_collective(p, "AllReduce", ALU.add, sts_l.ap().opt(), sts_r.ap().opt())
        with nc.Block() as blk:
            @blk.sync
            def _(e):
                p.dma(e, rks[:, :], sts_r.ap()[:, :])
        with nc.Block() as blk:
            @blk.vector
            def _(v):
                p.wait(v, "sd")
                v.reciprocal(rks[:, :], rks[:, :])
            @blk.sync
            def _(e):
                pass

        with ExitStack() as es2:
            psc = es2.enter_context(nc.psum_tensor("l0a_psc", [128, 2, 512], F32))

            def cx_mm(t, i, b):
                cd, eb = divmod(i, 2)
                for tc in range(TC):
                    ins = t.matmul(psc[:, b, :],
                                   KET[:, tc, cd * 128:(cd + 1) * 128],
                                   VT[:, tc, eb * 512:(eb + 1) * 512],
                                   start=(tc == 0), stop=(tc == TC - 1))
                return ins

            def cx_st(e, i, b):
                cd, eb = divmod(i, 2)
                return e.dma_start(out=ctx_loc.ap()[cd * 128:(cd + 1) * 128,
                                                    eb * 512:(eb + 1) * 512],
                                   in_=CSTG[:, b, :])

            pipe(p, "l0acx", DCH * 2, [
                ("tensor", cx_mm, 1),
                ("vector", lambda v, i, b: v.tensor_copy(CSTG[:, b, :], psc[:, b, :]), 1),
                ("sync", cx_st, 16),
            ], nbuf=2)
        emit_collective(p, "AllReduce", ALU.add, ctx_loc.ap().opt(), ctx_red.ap().opt())

        def sc_ld(e, i, b):
            return e.dma_start(out=CFR[:, b, :], in_=ctx_red.ap()[i * 128:(i + 1) * 128, :])

        def sc_ts(v, i, b):
            return v.tensor_scalar(CTXB[:, i, :], CFR[:, b, :], rks[:, i:i + 1], None,
                                   op0=ALU.mult)

        pipe(p, "l0asc", DCH, [("sync", sc_ld, 16), ("vector", sc_ts, 1)], nbuf=2)

        with ExitStack() as es2:
            psa = es2.enter_context(nc.psum_tensor("l0a_psa", [128, 2, 512], F32))

            def at_mm(t, i, b):
                ce, tb = divmod(i, TB)
                for cd in range(DCH):
                    ins = t.matmul(psa[:, b, :],
                                   CTXB[:, cd, ce * 128:(ce + 1) * 128],
                                   QSMT[:, cd, tb * 512:(tb + 1) * 512],
                                   start=(cd == 0), stop=(cd == DCH - 1))
                return ins

            pipe(p, "l0aat", DCH * TB, [
                ("tensor", at_mm, 1),
                ("vector", lambda v, i, b: v.tensor_copy(
                    ATT[:, i // TB, (i % TB) * 512:((i % TB) + 1) * 512], psa[:, b, :]), 1),
            ], nbuf=2)

        emit_w_load(p, WO2, Wo)
        with ExitStack() as es2:
            pso = es2.enter_context(nc.psum_tensor("l0a_pso", [128, 2, 512], F32))
            hrec = HREC

            def o_mm(t, i, b):
                cd, tb = divmod(i, TB)
                for ce in range(DCH):
                    ins = t.matmul(pso[:, b, :],
                                   WO2[:, ce, cd * 128:(cd + 1) * 128],
                                   ATT[:, ce, tb * 512:(tb + 1) * 512],
                                   start=(ce == 0), stop=(ce == DCH - 1))
                return ins

            def o_s(v, i, b):
                cd, tb = divmod(i, TB)
                sl = slice(tb * 512, (tb + 1) * 512)
                v.tensor_tensor(hrec[:, b, :], XT3[:, cd, sl], env["mbc"][:, sl],
                                op=ALU.subtract)
                v.tensor_tensor(hrec[:, b, :], hrec[:, b, :], env["rbc"][:, sl], op=ALU.mult)
                return v.scalar_tensor_tensor(SSTG[:, b, :], pso[:, b, :],
                                              bo_pc[:, cd:cd + 1], hrec[:, b, :],
                                              op0=ALU.add, op1=ALU.add)

            def o_st(e, i, b):
                cd, tb = divmod(i, TB)
                return e.dma_start(out=s_scr.ap()[cd * 128:(cd + 1) * 128,
                                                  tb * 512:(tb + 1) * 512],
                                   in_=SSTG[:, b, :])

            pipe(p, "l0ao", DCH * TB, [
                ("tensor", o_mm, 1),
                ("vector", o_s, 1),
                ("sync", o_st, 16),
            ], nbuf=2)

        if True:
            sc = ln_scratch(B)
            sums, sqs, var = sc["sums"], sc["sqs"], sc["var"]

            def g_ld(e, k, b):
                tb, cd = divmod(k, DCH)
                return e.dma_start(out=SRLD[:, b, :],
                                   in_=s_scr.ap()[cd * 128:(cd + 1) * 128,
                                                  tb * 512:(tb + 1) * 512])

            emit_ln_rows(p, env, "l0ag", lambda k, b: SRLD[:, b, :], TB * DCH,
                         lambda k: (k // DCH, k % DCH), False, C_LNEPS, sums, sqs, var,
                         sqb=sc["sqb"], dma_stage=g_ld)
            emit_bc(p, env, sums, env["mbc"], "l0agm")
            emit_bc(p, env, var, env["rbc"], "l0agr")

            def e_ld(e, k, b):
                tb, cd = divmod(k, DCH)
                return e.dma_start(out=SRLD[:, b, :],
                                   in_=s_scr.ap()[cd * 128:(cd + 1) * 128,
                                                  tb * 512:(tb + 1) * 512])

            def e_app(v, k, b):
                tb, cd = divmod(k, DCH)
                sl = slice(tb * 512, (tb + 1) * 512)
                v.tensor_tensor(SRLD[:, b, :], SRLD[:, b, :], env["mbc"][:, sl],
                                op=ALU.subtract)
                v.tensor_tensor(SRLD[:, b, :], SRLD[:, b, :], env["rbc"][:, sl], op=ALU.mult)
                v.tensor_scalar(SRLD[:, b, :], SRLD[:, b, :], ga_pc[:, cd:cd + 1],
                                be_pc[:, cd:cd + 1], op0=ALU.mult, op1=ALU.add)
                return v.tensor_tensor(XT3[:, cd, sl], XT3[:, cd, sl], SRLD[:, b, :],
                                       op=ALU.add)

            pipe(p, "l0ae", TB * DCH, [("sync", e_ld, 16), ("vector", e_app, 1)], nbuf=2)



EMIT = {"l0a": emit_l0_attn}
BLOCK_ORDER = ["l0a", "l0f", "l1a", "l1f"]


def make_consts():
    inv_freq = (10000.0 ** (-(np.arange(0, 64, 2, dtype=np.float64) / 64.0)))
    t = np.arange(SQ, dtype=np.float64)
    prow = np.arange(128)
    freq = inv_freq[(prow % 64) // 2]
    ang = freq[:, None] * t[None, :]
    ropec = np.cos(ang).astype(np.float32)
    sgn = np.where(prow % 2 == 1, 1.0, -1.0)
    ropes = (np.sin(ang) * sgn[:, None]).astype(np.float32)
    mask = np.where(np.arange(128)[:, None] >= np.arange(128)[None, :],
                    3e38, -3e38).astype(np.float32)
    e4 = np.zeros((4, 512), dtype=np.float32)
    for _k in range(4):
        e4[_k, _k * 128:(_k + 1) * 128] = 1.0
    e8 = np.zeros((8, 1024), dtype=np.float32)
    for _k in range(8):
        e8[_k, _k * 128:(_k + 1) * 128] = 1.0
    idb = np.eye(128, dtype=BF)
    idf = np.eye(128, dtype=np.float32)
    ones1 = np.ones((1, 128), dtype=np.float32)
    onesc = np.ones((128, 1), dtype=np.float32)
    cst = np.zeros((128, 8), dtype=np.float32)
    cst[:, C_MAGIC] = MAGIC
    cst[:, C_LNEPS] = LN_EPS
    cst[:, C_RMSEPS] = RMS_EPS
    cst[:, C_INV1024] = 1.0 / 1024.0
    cst[:, C_WSEPS] = 1e-5
    cst[:, C_ONE] = 1.0
    cst[:, C_INVHD] = 0.125
    cst[:, C_INV4M] = 1.0 / (4096.0 * 1024.0)
    return dict(idb=idb, idf=idf, ones1=ones1, onesc=onesc, cst=cst,
                ropec=ropec, ropes=ropes, mask=mask, e4=e4, e8=e8)


def build_program(blocks):
    nc = bass.Bass("TRN2", target_bir_lowering=False)
    p = Prog(nc)
    consts = make_consts()
    cdr = {k: nc.inline_tensor(v, name=f"c_{k}") for k, v in consts.items()}
    x_out = nc.dram_tensor("xO", [DM, TL], F32, kind="ExternalOutput")
    with p.es:
        env = emit_setup(p, cdr)
        for b in blocks:
            EMIT[b](p, env)
        emit_store_x(p, env, x_out)
    return nc


def vec_pc(v):
    return np.ascontiguousarray(np.asarray(v, dtype=np.float32).reshape(DCH, 128).T)


def stage_inputs(blocks, inputs, core):
    b, par = core // 2, core % 2
    m = {}
    x = np.asarray(inputs["L0_attn_x"], dtype=np.float32)
    m["xT"] = np.ascontiguousarray(x[b, par * TL:(par + 1) * TL, :].T)
    if "l0a" in blocks:
        m["l0a_wq"] = np.ascontiguousarray(np.asarray(inputs["L0_attn_Wq"], np.float32))
        m["l0a_wk"] = np.ascontiguousarray(np.asarray(inputs["L0_attn_Wk"], np.float32))
        m["l0a_wv"] = np.ascontiguousarray(np.asarray(inputs["L0_attn_Wv"], np.float32))
        m["l0a_wo"] = np.ascontiguousarray(np.asarray(inputs["L0_attn_Wo"], np.float32))
        m["l0a_vecs"] = np.ascontiguousarray(np.concatenate(
            [vec_pc(inputs["L0_attn_bo"]), vec_pc(inputs["L0_attn_gamma"]),
             vec_pc(inputs["L0_attn_beta"])], axis=1))
    if "l0f" in blocks:
        tau = np.zeros(64, dtype=np.int64)
        for _i in range(32):
            tau[2 * _i] = _i
            tau[2 * _i + 1] = 32 + _i
        qcols = np.concatenate([64 * (8 * par + j) + tau for j in range(8)])
        kcols = np.concatenate([64 * (4 * par + j // 2) + tau for j in range(8)])
        vcols = np.concatenate([64 * (4 * par + j) + np.arange(64) for j in range(4)])
        orows = np.concatenate([64 * (8 * par + j) + np.arange(64) for j in range(8)])
        m["l0f_wq"] = np.ascontiguousarray(np.asarray(inputs["L0_ffn_wq"], np.float32)[:, qcols])
        m["l0f_wk"] = np.ascontiguousarray(np.asarray(inputs["L0_ffn_wk"], np.float32)[:, kcols])
        m["l0f_wv"] = np.ascontiguousarray(np.asarray(inputs["L0_ffn_wv"], np.float32)[:, vcols])
        m["l0f_wo"] = np.ascontiguousarray(np.asarray(inputs["L0_ffn_wo"], np.float32)[orows, :])
        m["l0f_lnw"] = np.ascontiguousarray(np.concatenate(
            [vec_pc(inputs["L0_ffn_ln1_w"]), vec_pc(inputs["L0_ffn_ln2_w"])], axis=1))
        m["l0f_rw"] = np.ascontiguousarray(np.asarray(inputs["L0_ffn_router_w"], np.float32))
        m["l0f_wg"] = np.ascontiguousarray(np.asarray(inputs["L0_ffn_w_gate"], np.float32))
        m["l0f_wu"] = np.ascontiguousarray(np.asarray(inputs["L0_ffn_w_up"], np.float32))
        m["l0f_wd"] = np.ascontiguousarray(np.asarray(inputs["L0_ffn_w_down"], np.float32))
    if "l1a" in blocks:
        sl = slice(512 * par, 512 * par + 512)
        Wq = np.asarray(inputs["L1_attn_Wq"], np.float32)
        Wk = np.asarray(inputs["L1_attn_Wk"], np.float32)
        Wv = np.asarray(inputs["L1_attn_Wv"], np.float32)
        Wo = np.asarray(inputs["L1_attn_Wo"], np.float32)
        m["l1a_wq"] = np.ascontiguousarray(Wq[sl, :].T)
        m["l1a_wk"] = np.ascontiguousarray(Wk[sl, :].T)
        m["l1a_wv"] = np.ascontiguousarray(Wv[sl, :].T)
        m["l1a_wo"] = np.ascontiguousarray(Wo[:, sl].T)

        def vec_pc4(v):
            return np.ascontiguousarray(np.asarray(v, np.float32).reshape(4, 128).T)
        m["l1a_vecs"] = np.ascontiguousarray(np.concatenate(
            [vec_pc4(np.asarray(inputs["L1_attn_bq"], np.float32)[sl]),
             vec_pc4(np.asarray(inputs["L1_attn_bk"], np.float32)[sl]),
             vec_pc(inputs["L1_attn_bo"])], axis=1))
        m["l1a_bv"] = np.ascontiguousarray(
            np.asarray(inputs["L1_attn_bv"], np.float32)[sl].reshape(1, 512))
    if "l1f" in blocks:
        m["l1f_w1"] = np.ascontiguousarray(np.asarray(inputs["L1_ffn_W1"], np.float32).T)
        m["l1f_w2"] = np.ascontiguousarray(np.asarray(inputs["L1_ffn_W2"], np.float32).T)
        m["l1f_b1"] = np.ascontiguousarray(
            np.asarray(inputs["L1_ffn_b1"], np.float32).reshape(1, 4096))
        m["l1f_vecs"] = vec_pc(inputs["L1_ffn_b2"])
    return m


def run_blocks(blocks, inputs, x_override=None):
    nc = build_program(blocks)
    in_maps = []
    for core in range(8):
        m = stage_inputs(blocks, inputs, core)
        if x_override is not None:
            m["xT"] = x_override[core]
        in_maps.append(m)
    res = run_bass_kernel_spmd(nc, in_maps, core_ids=list(range(8)))
    return [r["xO"] for r in res.results]


FUSED = False


def kernel(**inputs):
    order = ["l0a", "l0f", "l1a", "l1f"]
    if FUSED:
        outs = run_blocks(order, inputs)
    else:
        outs = None
        for blk in order:
            outs = run_blocks([blk], inputs, x_override=outs)
    x = np.zeros((NB, SQ, DM), dtype=np.float32)
    for core in range(8):
        b, par = core // 2, core % 2
        x[b, par * TL:(par + 1) * TL, :] = outs[core].T
    return x



def emit_causal_attn(p, env, cfg):
    nc = p.nc
    pfx = cfg["pfx"]
    XT, A, B, C, D, SM = env["XT"], env["A"], env["B"], env["C"], env["D"], env["SM"]
    kch, vch = cfg["kch"], cfg["vch"]
    kd, vd = kch * 128, vch * 128
    QCH = 4

    hag_l = cfg["hag_l"]
    hag = nc.dram_tensor(f"{pfx}_hag", [2 * DM, TL], BF16, kind="Internal")
    if DEBUG_ATTN:
        dbg_q = nc.dram_tensor(f"{pfx}_dbgq", [512, SQ], BF16, kind="ExternalOutput")
        dbg_k = nc.dram_tensor(f"{pfx}_dbgk", [kd, SQ], BF16, kind="ExternalOutput")
        dbg_v = nc.dram_tensor(f"{pfx}_dbgv", [SQ, vd], BF16, kind="ExternalOutput")
        dbg_av = nc.dram_tensor(f"{pfx}_dbgav", [512, SQ], BF16, kind="ExternalOutput")
        dbg_op = nc.dram_tensor(f"{pfx}_dbgop", [SQ, DM], F32, kind="ExternalOutput")
    q_scr = nc.dram_tensor(f"{pfx}_qscr", [512, SQ], BF16, kind="Internal")
    op_l = nc.dram_tensor(f"{pfx}_opl", [SQ, DM], F32, kind="Internal")
    om = nc.dram_tensor(f"{pfx}_om", [TL, DM], F32, kind="Internal")

    COS = fv3(XT, 0, 1, SQ)[:, 0, :]
    SIN = fv3(XT, 4096, 1, SQ)[:, 0, :]
    RSH = fv3(XT, 8192, 2, 512)
    QSG = bv3(XT, 9216, QCH, 512)
    BVBC = fv(XT, 10240, vd)
    KT3 = bv3(A, 0, kch, SQ)
    VTOK = bv3(B, 0, 32, vd)
    WQB = bv3(C, 0, DCH, 512)
    WKB = bv3(C, 2048, DCH, kd)
    WVB = bv3(C, 2048 + DCH * kd // 2, DCH, vd)
    AVT = bv3(C, 0, QCH, SQ)
    HAGT = bv3(D, 0, 2, DCH * 512)
    WOB = bv3(D, 0, QCH, DM)
    OMR = fv3(D, 2048, 2, DM)

    PBT = bv3(XT, 0, 2, 32 * 256)
    PRB = bv3(XT, 8192, 2, SQ)
    QHP = bv3(XT, 12288, 1, SQ)[:, 0, :]

    with ExitStack() as es:
        smx = es.enter_context(nc.sbuf_tensor(f"{pfx}_smx", [128, 16], F32))
        srw = es.enter_context(nc.sbuf_tensor(f"{pfx}_srw", [128, 4], F32))
        rcp = es.enter_context(nc.sbuf_tensor(f"{pfx}_rcp", [128, 4], F32))
        ngm = es.enter_context(nc.sbuf_tensor(f"{pfx}_ngm", [128, 4], F32))
        dgb = es.enter_context(nc.sbuf_tensor(f"{pfx}_dgb", [128, 2, 128], BF16))
        mskb = es.enter_context(nc.sbuf_tensor(f"{pfx}_mskb", [128, 128], F32))


        with nc.Block() as blk:
            @blk.sync
            def _(e):
                p.dma(e, fv(XT, 0, SQ), env["cdr"]["ropec"][:, :])
                p.dma(e, fv(XT, 4096, SQ), env["cdr"]["ropes"][:, :])
                p.dma(e, mskb[:, :], env["cdr"]["mask"][:, :])
                if cfg.get("bv") is not None:
                    p.dma(e, fv(SM, 0, vd)[0:1, :], cfg["bv"].ap()[:, :])
            @blk.gpsimd
            def _(g):
                p.dma(g, WQB[:, :, :], cfg["wq"].ap().rearrange("(c p) n -> p c n", p=128), q="gd")
                p.dma(g, WKB[:, :, :], cfg["wk"].ap().rearrange("(c p) n -> p c n", p=128), q="gd")
                p.dma(g, WVB[:, :, :], cfg["wv"].ap().rearrange("(c p) n -> p c n", p=128), q="gd")
                p.wait(g, "gd")
                p.wait(g, "sd")
        if cfg.get("bv") is not None:
            with ExitStack() as es2:
                psb = es2.enter_context(nc.psum_tensor(f"{pfx}_psbv", [128, 512], F32))
                with nc.Block() as blk:
                    @blk.tensor
                    def _(t):
                        p.inc(t.matmul(psb[:, 0:vd], env["ones1"][:, :], fv(SM, 0, vd)[0:1, :],
                                       start=True, stop=True), f"{pfx}bv", 1)
                    @blk.vector
                    def _(v):
                        p.wait(v, f"{pfx}bv")
                        v.tensor_copy(BVBC[:, :], psb[:, 0:vd])

        emit_collective(p, "AllGather", ALU.bypass, hag_l.ap().opt(), hag.ap().opt())

        with ExitStack() as es2:
            psp = es2.enter_context(nc.psum_tensor(f"{pfx}_psp", [128, 4, 512], F32))
            NG = QCH + kch + 4
            b_hg = p.base(f"{pfx}hg"); b_pm = p.base(f"{pfx}pm"); b_pe = p.base(f"{pfx}pe")
            b_qs = p.base(f"{pfx}qs")
            with nc.Block() as blk:
                @blk.sync
                def _(e):
                    for tg in range(8):
                        if tg >= 2:
                            e.wait_ge(p.S(f"{pfx}pm"), b_pm + (tg - 1) * NG)
                        r = tg // 4
                        p.inc(e.dma_start(
                            out=HAGT[:, tg % 2, :].rearrange("p (c n) -> p c n", c=DCH),
                            in_=hag.ap()[r * DM:(r + 1) * DM,
                                         (tg % 4) * 512:((tg % 4) + 1) * 512]
                                .rearrange("(c p) n -> p c n", p=128)), f"{pfx}hg", 16)
                        if tg >= 1:
                            e.wait_ge(p.S(f"{pfx}pe"), b_pe + (tg - 1) * NG + QCH)
                            p.inc(e.dma_start(
                                out=q_scr.ap()[:, (tg - 1) * 512:tg * 512]
                                    .rearrange("(c p) n -> p c n", p=128),
                                in_=QSG[:, :, :]), f"{pfx}qs", 16)
                    e.wait_ge(p.S(f"{pfx}pe"), b_pe + 8 * NG)
                    p.inc(e.dma_start(
                        out=q_scr.ap()[:, 7 * 512:8 * 512]
                            .rearrange("(c p) n -> p c n", p=128),
                        in_=QSG[:, :, :]), f"{pfx}qs", 16)
                @blk.tensor
                def _(t):
                    for tg in range(8):
                        t.wait_ge(p.S(f"{pfx}hg"), b_hg + (tg + 1) * 16)
                        hg = HAGT[:, tg % 2, :].rearrange("p (c n) -> p c n", c=DCH)
                        g = 0
                        for kind, n_out in (("q", QCH), ("k", kch), ("v", 4)):
                            for j in range(n_out):
                                gi = tg * NG + g
                                if gi >= 4:
                                    t.wait_ge(p.S(f"{pfx}pe"), b_pe + gi - 3)
                                pb = psp[:, gi % 4, :]
                                for c in range(DCH):
                                    if kind == "q":
                                        ins = t.matmul(pb, WQB[:, c, j * 128:(j + 1) * 128],
                                                       hg[:, c, :], start=(c == 0),
                                                       stop=(c == DCH - 1))
                                    elif kind == "k":
                                        ins = t.matmul(pb, WKB[:, c, j * 128:(j + 1) * 128],
                                                       hg[:, c, :], start=(c == 0),
                                                       stop=(c == DCH - 1))
                                    else:
                                        ins = t.matmul(pb[:, 0:vd], hg[:, c, j * 128:(j + 1) * 128],
                                                       WVB[:, c, :], start=(c == 0),
                                                       stop=(c == DCH - 1))
                                p.inc(ins, f"{pfx}pm", 1)
                                g += 1
                @blk.vector
                def _(v):
                    for tg in range(8):
                        g = 0
                        for kind, n_out in (("q", QCH), ("k", kch), ("v", 4)):
                            for j in range(n_out):
                                gi = tg * NG + g
                                v.wait_ge(p.S(f"{pfx}pm"), b_pm + gi + 1)
                                pb = psp[:, gi % 4, :]
                                csl = slice(tg * 512, (tg + 1) * 512)
                                if kind == "q":
                                    if tg >= 1 and j == 0:
                                        v.wait_ge(p.S(f"{pfx}qs"), b_qs + tg * 16)
                                    if cfg.get("bq") is not None:
                                        v.tensor_scalar(RSH[:, 0, :], pb, cfg["bq"][:, j:j + 1],
                                                        None, op0=ALU.add)
                                    else:
                                        v.tensor_copy(RSH[:, 0, :], pb)
                                    v.stream_shuffle(RSH[:, 1, :], RSH[:, 0, :], PAIRSWAP)
                                    v.tensor_tensor(RSH[:, 0, :], RSH[:, 0, :], COS[:, csl],
                                                    op=ALU.mult)
                                    v.tensor_tensor(RSH[:, 1, :], RSH[:, 1, :], SIN[:, csl],
                                                    op=ALU.mult)
                                    ins = v.tensor_tensor(QSG[:, j, :], RSH[:, 0, :],
                                                          RSH[:, 1, :], op=ALU.add)
                                elif kind == "k":
                                    if cfg.get("bk") is not None:
                                        v.tensor_scalar(RSH[:, 0, :], pb, cfg["bk"][:, j:j + 1],
                                                        None, op0=ALU.add)
                                    else:
                                        v.tensor_copy(RSH[:, 0, :], pb)
                                    v.stream_shuffle(RSH[:, 1, :], RSH[:, 0, :], PAIRSWAP)
                                    v.tensor_tensor(RSH[:, 0, :], RSH[:, 0, :], COS[:, csl],
                                                    op=ALU.mult)
                                    v.tensor_tensor(RSH[:, 1, :], RSH[:, 1, :], SIN[:, csl],
                                                    op=ALU.mult)
                                    ins = v.tensor_tensor(KT3[:, j, csl], RSH[:, 0, :],
                                                          RSH[:, 1, :], op=ALU.add)
                                else:
                                    dst = VTOK[:, tg * 4 + j, :]
                                    if cfg.get("bv") is not None:
                                        ins = v.tensor_tensor(dst, pb[:, 0:vd], BVBC[:, :],
                                                              op=ALU.add)
                                    else:
                                        ins = v.tensor_copy(dst, pb[:, 0:vd])
                                p.inc(ins, f"{pfx}pe", 1)
                                g += 1

        if DEBUG_ATTN:
            with nc.Block() as blk:
                @blk.sync
                def _(e):
                    p.dma(e, dbg_k.ap().rearrange("(c p) n -> p c n", p=128), KT3[:, :, :])
                    p.dma(e, dbg_v.ap().rearrange("(t p) n -> p t n", p=128), VTOK[:, :, :])
                    p.dma(e, dbg_q.ap(), q_scr.ap())
                    p.wait(e, "sd")

        for hp in range(4):
            emit_attn_hp(p, env, cfg, hp, dict(q_scr=q_scr, KT3=KT3, VTOK=VTOK, AVT=AVT,
                                               PBT=PBT, PRB=PRB, QHP=QHP, smx=smx, srw=srw,
                                               rcp=rcp, ngm=ngm, dgb=dgb, mskb=mskb))

        if DEBUG_ATTN:
            with nc.Block() as blk:
                @blk.sync
                def _(e):
                    p.dma(e, dbg_av.ap().rearrange("(c p) n -> p c n", p=128), AVT[:, :, :])
                    p.wait(e, "sd")

        emit_w_load(p, WOB, cfg["wo"], nchunk=QCH, cols=DM)
        with ExitStack() as es2:
            pso = es2.enter_context(nc.psum_tensor(f"{pfx}_pso", [128, 2, 512], F32))

            def op_mm(t, i, b):
                tcg, db = divmod(i, 2)
                for e in range(QCH):
                    ins = t.matmul(pso[:, b, :], AVT[:, e, tcg * 128:(tcg + 1) * 128],
                                   WOB[:, e, db * 512:(db + 1) * 512],
                                   start=(e == 0), stop=(e == QCH - 1))
                return ins

            OSTG = fv3(env["SM"], 0, 2, 512)

            def op_st(e, i, b):
                tcg, db = divmod(i, 2)
                return e.dma_start(out=op_l.ap()[tcg * 128:(tcg + 1) * 128,
                                                 db * 512:(db + 1) * 512],
                                   in_=OSTG[:, b, :])

            pipe(p, f"{pfx}op", 64, [
                ("tensor", op_mm, 1),
                ("vector", lambda v, i, b: v.tensor_copy(OSTG[:, b, :], pso[:, b, :]), 1),
                ("sync", op_st, 16),
            ], nbuf=2)

        if DEBUG_ATTN:
            with nc.Block() as blk:
                @blk.sync
                def _(e):
                    p.dma(e, dbg_op.ap(), op_l.ap())
                    p.wait(e, "sd")
        emit_collective(p, "ReduceScatter", ALU.add, op_l.ap().opt(), om.ap().opt())
    return om, OMR


def emit_attn_hp(p, env, cfg, hp, bufs):
    nc = p.nc
    pfx = cfg["pfx"]
    q_scr, KT3, VTOK = bufs["q_scr"], bufs["KT3"], bufs["VTOK"]
    AVT, PBT, PRB, QHP = bufs["AVT"], bufs["PBT"], bufs["PRB"], bufs["QHP"]
    smx, srw, rcp, ngm = bufs["smx"], bufs["srw"], bufs["rcp"], bufs["ngm"]
    dgb, mskb = bufs["dgb"], bufs["mskb"]

    items = [(s, jj, qb) for s in range(16) for jj in range(2) for qb in range(2)]
    NIT = len(items)

    def gblk(ii):
        s, jj, qb = items[ii]
        return 2 * s + qb

    def nkb(ii):
        return (gblk(ii) + 4) // 4

    kb_base = [0]
    for ii in range(NIT):
        kb_base.append(kb_base[-1] + nkb(ii))

    tgroups = []
    item_g0, item_gend = {}, {}
    for ii in range(NIT):
        item_g0[ii] = len(tgroups)
        for k0 in range(0, gblk(ii) + 1, 4):
            tgroups.append((ii, k0, min(4, gblk(ii) + 1 - k0)))
        item_gend[ii] = len(tgroups)
    av_need = [item_gend[4 * s + 3] for s in range(16)]

    def krow_of(ii):
        s, jj, qb = items[ii]
        return KT3[jj * 64:jj * 64 + 64, hp, :]

    def vofs_of(jj):
        return (hp if cfg["kv_shared"] else 2 * hp + jj) * 64

    with ExitStack() as es:
        psc = es.enter_context(nc.psum_tensor(f"{pfx}_psc{hp}", [128, 4, 512], F32))
        pst = es.enter_context(nc.psum_tensor(f"{pfx}_pst{hp}", [128, 2, 512], F32))
        psv = es.enter_context(nc.psum_tensor(f"{pfx}_psv{hp}", [128, 2, 512], F32))
        scb = fv(env["D"], 0, SQ)

        b_qh = p.base(f"{pfx}qh")
        b_as = p.base(f"{pfx}as")
        b_ae = p.base(f"{pfx}ae")
        b_ax = p.base(f"{pfx}ax")
        b_ar = p.base(f"{pfx}ar")
        b_at = p.base(f"{pfx}at")
        b_ap_ = p.base(f"{pfx}ap")
        b_av = p.base(f"{pfx}av")
        b_ao = p.base(f"{pfx}ao")

        def emit_tmm(t, ii):
            t.wait_ge(p.S(f"{pfx}ar"), b_ar + ii + 1)
            for gi in range(item_g0[ii], item_gend[ii]):
                _, k0, kc = tgroups[gi]
                if gi >= 2:
                    t.wait_ge(p.S(f"{pfx}ap"), b_ap_ + gi - 1)
                for k in range(kc):
                    ins = t.matmul(pst[:, gi % 2, k * 128:(k + 1) * 128],
                                   PRB[:, ii % 2, (k0 + k) * 128:(k0 + k + 1) * 128],
                                   dgb[:, ii % 2, :], start=True, stop=True)
                p.inc(ins, f"{pfx}at", 1)

        def emit_avmm(t, s):
            t.wait_ge(p.S(f"{pfx}ap"), b_ap_ + av_need[s])
            if s >= 2:
                t.wait_ge(p.S(f"{pfx}ao"), b_ao + s - 1)
            kcm = 2 * s + 2
            for jj in range(2):
                vofs = vofs_of(jj)
                pdst = psv[jj * 64:jj * 64 + 64, s % 2, 0:256]
                pbt = PBT[:, jj, :].rearrange("p (kc n) -> p kc n", kc=32)
                for kc in range(kcm):
                    ins = t.matmul(pdst, VTOK[:, kc, vofs:vofs + 64], pbt[:, kc, 0:256],
                                   start=(kc == 0), stop=(kc == kcm - 1))
            p.inc(ins, f"{pfx}av", 1)

        def emit_recip_diag(v, ii):
            v.wait_ge(p.S(f"{pfx}ax"), b_ax + ii + 1)
            if ii >= 2:
                v.wait_ge(p.S(f"{pfx}at"), b_at + item_gend[ii - 2])
            p.inc(v.reciprocal(rcp[:, ii % 4:ii % 4 + 1], srw[:, ii % 4:ii % 4 + 1]),
                  f"{pfx}rc", 1)
            v.wait_ge(p.S(f"{pfx}rc"), p.cnt[f"{pfx}rc"])
            p.inc(v.tensor_scalar(dgb[:, ii % 2, :], env["idb"][:, :],
                                  rcp[:, ii % 4:ii % 4 + 1], None, op0=ALU.mult),
                  f"{pfx}ar", 1)

        def emit_ptev(v, ii):
            s, jj, qb = items[ii]
            if s >= 1:
                v.wait_ge(p.S(f"{pfx}av"), b_av + s)
            dst = PBT[:, jj, :].rearrange("p (kc n) -> p kc n", kc=32)
            if qb == 0:
                v.memset(dst[:, 2 * s + 1, 0:128], 0)
            for gi in range(item_g0[ii], item_gend[ii]):
                _, k0, kc = tgroups[gi]
                v.wait_ge(p.S(f"{pfx}at"), b_at + gi + 1)
                p.inc(v.tensor_copy(
                    dst[:, k0:k0 + kc, qb * 128:qb * 128 + 128],
                    pst[:, gi % 2, 0:kc * 128].rearrange("p (kc n) -> p kc n", kc=kc)),
                    f"{pfx}ap", 1)

        def emit_avev(v, s):
            v.wait_ge(p.S(f"{pfx}av"), b_av + s + 1)
            p.inc(v.tensor_copy(AVT[:, hp, s * 256:(s + 1) * 256], psv[:, s % 2, 0:256]),
                  f"{pfx}ao", 1)

        with nc.Block() as blk:
            @blk.sync
            def _(e):
                p.inc(e.dma_start(out=QHP[:, :],
                                  in_=q_scr.ap()[hp * 128:(hp + 1) * 128, :]),
                      f"{pfx}qh", 16)
            @blk.tensor
            def _(t):
                t.wait_ge(p.S(f"{pfx}qh"), b_qh + 16)
                for ii in range(NIT):
                    g = gblk(ii)
                    s, jj, qb = items[ii]
                    qrow = QHP[jj * 64:jj * 64 + 64, g * 128:(g + 1) * 128]
                    krow = krow_of(ii)
                    for kb in range(nkb(ii)):
                        j = kb_base[ii] + kb
                        if j >= 4:
                            t.wait_ge(p.S(f"{pfx}ae"), b_ae + j - 3)
                        w = min(512, (g + 1) * 128 - kb * 512)
                        p.inc(t.matmul(psc[:, j % 4, 0:w], qrow,
                                       krow[:, kb * 512:kb * 512 + w],
                                       start=True, stop=True), f"{pfx}as", 1)
                    if ii >= 5 and ii % 4 == 1:
                        emit_avmm(t, ii // 4 - 1)
                    if ii >= 1:
                        emit_tmm(t, ii - 1)
                emit_tmm(t, NIT - 1)
                emit_avmm(t, 15)
            @blk.vector
            def _(v):
                for ii in range(NIT):
                    g = gblk(ii)
                    nb = nkb(ii)
                    if ii >= 1:
                        v.wait_ge(p.S(f"{pfx}ax"), b_ax + ii)
                    for kb in range(nb):
                        j = kb_base[ii] + kb
                        v.wait_ge(p.S(f"{pfx}as"), b_as + j + 1)
                        w = min(512, (g + 1) * 128 - kb * 512)
                        wpre = w if kb < nb - 1 else w - 128
                        ins = None
                        if wpre > 0:
                            ins = v.tensor_scalar(scb[:, kb * 512:kb * 512 + wpre],
                                                  psc[:, j % 4, 0:wpre], 0.0, None,
                                                  op0=ALU.add, op1=ALU.max,
                                                  accum_out=smx[:, kb:kb + 1])
                        if kb == nb - 1:
                            dofs = g * 128 - kb * 512
                            dcol = nb if wpre > 0 else nb - 1
                            v.tensor_tensor(scb[:, g * 128:(g + 1) * 128],
                                            psc[:, j % 4, dofs:dofs + 128],
                                            mskb[:, :], op=ALU.min)
                            p.inc(v.tensor_scalar(scb[:, g * 128:(g + 1) * 128],
                                                  scb[:, g * 128:(g + 1) * 128], 0.0,
                                                  None, op0=ALU.add, op1=ALU.max,
                                                  accum_out=smx[:, dcol:dcol + 1]),
                                  f"{pfx}ch", 1)
                            v.wait_ge(p.S(f"{pfx}ch"), p.cnt[f"{pfx}ch"])
                            p.inc(v.reduce_max(ngm[:, ii % 4:ii % 4 + 1],
                                               smx[:, 0:dcol + 1], axis=AX.X,
                                               negate=True), f"{pfx}ch", 1)
                            v.wait_ge(p.S(f"{pfx}ch"), p.cnt[f"{pfx}ch"])
                            ins = v.tensor_scalar(ngm[:, ii % 4:ii % 4 + 1],
                                                  ngm[:, ii % 4:ii % 4 + 1], 0.125,
                                                  None, op0=ALU.mult)
                        p.inc(ins, f"{pfx}ae", 1)
                    if ii >= 1:
                        emit_recip_diag(v, ii - 1)
                        emit_ptev(v, ii - 1)
                    if ii >= 6 and ii % 4 == 2:
                        emit_avev(v, ii // 4 - 1)
                emit_recip_diag(v, NIT - 1)
                emit_ptev(v, NIT - 1)
                emit_avev(v, 15)
            @blk.scalar
            def _(a):
                for ii in range(NIT):
                    g = gblk(ii)
                    a.wait_ge(p.S(f"{pfx}ae"), b_ae + kb_base[ii] + nkb(ii))
                    if ii >= 2:
                        a.wait_ge(p.S(f"{pfx}at"), b_at + item_gend[ii - 2])
                    p.inc(a.activation(PRB[:, ii % 2, 0:(g + 1) * 128],
                                       scb[:, 0:(g + 1) * 128], AF.Exp,
                                       bias=ngm[:, ii % 4:ii % 4 + 1],
                                       scale=env["cst"][:, C_INVHD:C_INVHD + 1],
                                       accum_out=srw[:, ii % 4:ii % 4 + 1]), f"{pfx}ax", 1)


def emit_apply_ln_stream(p, env, src3, dst_dram, tag):
    nc = p.nc
    HST = bv3(env["SM"], 0, 2, 512)
    TMP = fv3(env["SM"], 512, 2, 512)

    def ap_ops(v, k, b):
        c, tb = divmod(k, TB)
        sl = slice(tb * 512, (tb + 1) * 512)
        v.tensor_tensor(TMP[:, b, :], src3[:, c, sl], env["mbc"][:, sl], op=ALU.subtract)
        return v.tensor_tensor(HST[:, b, :], TMP[:, b, :], env["rbc"][:, sl], op=ALU.mult)

    def ap_st(e, k, b):
        c, tb = divmod(k, TB)
        return e.dma_start(out=dst_dram.ap()[c * 128:(c + 1) * 128,
                                             tb * 512:(tb + 1) * 512],
                           in_=HST[:, b, :])

    pipe(p, tag, DCH * TB, [("vector", ap_ops, 1), ("sync", ap_st, 16)], nbuf=2)


def emit_l1_attn(p, env):
    nc = p.nc
    XT3 = xt3(env["XT"])
    wq = nc.dram_tensor("l1a_wq", [DM, 512], F32, kind="ExternalInput")
    wk = nc.dram_tensor("l1a_wk", [DM, 512], F32, kind="ExternalInput")
    wv = nc.dram_tensor("l1a_wv", [DM, 512], F32, kind="ExternalInput")
    wo = nc.dram_tensor("l1a_wo", [512, DM], F32, kind="ExternalInput")
    vecs = nc.dram_tensor("l1a_vecs", [128, 16], F32, kind="ExternalInput")
    bv = nc.dram_tensor("l1a_bv", [1, 512], F32, kind="ExternalInput")
    hag_l = None

    with ExitStack() as es:
        vpc = es.enter_context(nc.sbuf_tensor("l1a_vpc", [128, 16], F32))
        with nc.Block() as blk:
            @blk.sync
            def _(e):
                p.dma(e, vpc[:, :], vecs[:, :])
                p.wait(e, "sd")

        emit_ln_full(p, env, XT3, rms=False, eps_col=C_LNEPS, tag="ln2", W=env["B"])
        cfg = dict(pfx="l1a", kch=4, vch=4, kv_shared=False, wq=wq, wk=wk, wv=wv, wo=wo,
                   bq=vpc[:, 0:4], bk=vpc[:, 4:8], bv=bv)
        hag_l = nc.dram_tensor("l1a_hagl2", [DM, TL], BF16, kind="Internal")
        cfg["hag_l"] = hag_l
        emit_apply_ln_stream(p, env, XT3, hag_l, "l1ah")
        emit_park_x(p, env)
        om, OMR = emit_causal_attn(p, env, cfg)
        emit_unpark_x(p, env)

        with ExitStack() as es2:
            psT = es2.enter_context(nc.psum_tensor("l1a_psT", [128, 2, 1024], F32))
            bo_pc = vpc[:, 8:16]

            def it_ld(e, i, b):
                return e.dma_start(out=OMR[:, b, :],
                                   in_=om.ap()[i * 128:(i + 1) * 128, :])

            def it_tr(t, i, b):
                for c in range(DCH):
                    ins = t.transpose(psT[:, b, c * 128:(c + 1) * 128],
                                      OMR[:, b, c * 128:(c + 1) * 128], env["idf"][:, :])
                return ins

            def it_add(v, i, b):
                for c in range(DCH):
                    sl = slice(i * 128, (i + 1) * 128)
                    ins = v.scalar_tensor_tensor(XT3[:, c, sl], psT[:, b, c * 128:(c + 1) * 128],
                                                 bo_pc[:, c:c + 1], XT3[:, c, sl],
                                                 op0=ALU.add, op1=ALU.add)
                return ins

            pipe(p, "l1ai", TC, [("sync", it_ld, 16), ("tensor", it_tr, 1),
                                 ("vector", it_add, 1)], nbuf=2)


EMIT["l1a"] = emit_l1_attn


def emit_l1_ffn(p, env):
    nc = p.nc
    XT, A, B, C, D, SM = env["XT"], env["A"], env["B"], env["C"], env["D"], env["SM"]
    XT3 = xt3(XT)
    w1 = nc.dram_tensor("l1f_w1", [DM, 4096], F32, kind="ExternalInput")
    w2 = nc.dram_tensor("l1f_w2", [4096, DM], F32, kind="ExternalInput")
    b1 = nc.dram_tensor("l1f_b1", [1, 4096], F32, kind="ExternalInput")
    vecs = nc.dram_tensor("l1f_vecs", [128, 8], F32, kind="ExternalInput")
    if DEBUG_ATTN:
        w1q = nc.dram_tensor("l1f_w1q", [DM, 4096], BF16, kind="ExternalOutput")
        w2q = nc.dram_tensor("l1f_w2q", [4096, DM], BF16, kind="ExternalOutput")
        dbg_xq = nc.dram_tensor("l1f_dbgxq", [TL, DM], BF16, kind="ExternalOutput")
        dbg_hs = nc.dram_tensor("l1f_dbghs", [512, 4096], F32, kind="ExternalOutput")
        dbg_xtok = nc.dram_tensor("l1f_dbgxtok", [TL, DM], F32, kind="ExternalOutput")
        dbg_st = nc.dram_tensor("l1f_dbgst", [128, 128], F32, kind="ExternalOutput")
    else:
        w1q = nc.dram_tensor("l1f_w1q", [DM, 4096], BF16, kind="Internal")
        w2q = nc.dram_tensor("l1f_w2q", [4096, DM], BF16, kind="Internal")

    with ExitStack() as es:
        vpc = es.enter_context(nc.sbuf_tensor("l1f_vpc", [128, 8], F32))
        wsb = es.enter_context(nc.sbuf_tensor("l1f_wsb", [128, 4], F32))
        asum = es.enter_context(nc.sbuf_tensor("l1f_asum", [128, 40], F32))
        st = es.enter_context(nc.sbuf_tensor("l1f_st", [128, 26], F32))
        rowb = es.enter_context(nc.sbuf_tensor("l1f_rowb", [1, 44], F32))
        e4s = env["mbc"][0:4, 1024:1536]
        with nc.Block() as blk:
            @blk.sync
            def _(e):
                p.dma(e, vpc[:, :], vecs[:, :])
                p.dma(e, e4s[:, :], env["cdr"]["e4"][:, :])
                p.wait(e, "sd")
        b2_pc = vpc[:, 0:8]

        WCH = fv3(A, 0, 2, 4096)
        for wi, (wd, nch, cw, a0) in enumerate(((w1, 8, 4096, 0), (w2, 32, 1024, 8))):
            def wl(e, i, b, wd=wd, cw=cw):
                return e.dma_start(out=WCH[:, b, 0:cw],
                                   in_=wd.ap()[i * 128:(i + 1) * 128, :])

            def wsum(v, i, b, cw=cw, a0=a0):
                return v.reduce_sum(asum[:, a0 + i:a0 + i + 1], WCH[:, b, 0:cw],
                                    axis=AX.X, apply_absolute_value=True)

            pipe(p, f"l1fws{wi}", nch, [("sync", wl, 16), ("vector", wsum, 1)], nbuf=2)
        with ExitStack() as es2:
            ps1 = es2.enter_context(nc.psum_tensor("l1f_ps1", [1, 512], F32))
            psb = es2.enter_context(nc.psum_tensor("l1f_psb", [128, 512], F32))
            with nc.Block() as blk:
                @blk.tensor
                def _(t):
                    p.inc(t.matmul(ps1[:, 0:40], env["onesc"][:, :], asum[:, :],
                                   start=True, stop=True), "l1fsm", 1)
                @blk.vector
                def _(v):
                    def step(ins):
                        p.inc(ins, "l1fch", 1)
                        v.wait_ge(p.S("l1fch"), p.cnt["l1fch"])
                    p.wait(v, "l1fsm")
                    step(v.tensor_copy(rowb[:, 0:40], ps1[:, 0:40]))
                    step(v.reduce_sum(rowb[:, 40:41], rowb[:, 0:8], axis=AX.X))
                    step(v.reduce_sum(rowb[:, 41:42], rowb[:, 8:40], axis=AX.X))
                    step(v.tensor_scalar(rowb[:, 40:42], rowb[:, 40:42],
                                         1.0 / (4096.0 * 1024.0), 1e-5,
                                         op0=ALU.mult, op1=ALU.max))
                    p.inc(v.reciprocal(rowb[:, 42:44], rowb[:, 40:42]), "l1frw", 1)
            with nc.Block() as blk:
                @blk.tensor
                def _(t):
                    p.wait(t, "l1frw")
                    p.inc(t.matmul(psb[:, 0:4], env["ones1"][:, :], rowb[:, 40:44],
                                   start=True, stop=True), "l1fbm", 1)
                @blk.vector
                def _(v):
                    p.wait(v, "l1fbm")
                    v.tensor_copy(wsb[:, 1:2], psb[:, 0:1])
                    v.tensor_copy(wsb[:, 3:4], psb[:, 1:2])
                    v.tensor_copy(wsb[:, 0:1], psb[:, 2:3])
                    p.inc(v.tensor_copy(wsb[:, 2:3], psb[:, 3:4]), "l1fwsb", 1)

        WQC = bv3(B, 0, 2, 4096)
        for wi, (wd, wqd, nch, cw, rwcol) in enumerate(
                ((w1, w1q, 8, 4096, 0), (w2, w2q, 32, 1024, 2))):
            def tl(e, i, b, wd=wd, cw=cw):
                return e.dma_start(out=WCH[:, b, 0:cw],
                                   in_=wd.ap()[i * 128:(i + 1) * 128, :])

            def tq(v, i, b, cw=cw, rwcol=rwcol):
                if i == 0:
                    p.wait(v, "l1fwsb")
                v.tensor_scalar(WCH[:, b, 0:cw], WCH[:, b, 0:cw], wsb[:, rwcol:rwcol + 1],
                                MAGIC, op0=ALU.mult, op1=ALU.add)
                v.tensor_scalar(WCH[:, b, 0:cw], WCH[:, b, 0:cw], MAGIC, -1.0,
                                op0=ALU.subtract, op1=ALU.max)
                return v.tensor_scalar(WQC[:, b, 0:cw], WCH[:, b, 0:cw], 1.0, None,
                                       op0=ALU.min)

            def ts_(e, i, b, wqd=wqd, cw=cw):
                return e.dma_start(out=wqd.ap()[i * 128:(i + 1) * 128, :],
                                   in_=WQC[:, b, 0:cw])

            pipe(p, f"l1ftq{wi}", nch, [("sync", tl, 16), ("vector", tq, 1),
                                        ("gpsimd", ts_, 16)], nbuf=2)

        XTKa = fv3(A, 0, 8, DM)
        XTKb = fv3(B, 0, 8, DM)

        def xtok(tc):
            return (XTKa if tc < 8 else XTKb)[:, tc % 8, :]

        with ExitStack() as es2:
            psT = es2.enter_context(nc.psum_tensor("l1f_psT", [128, 2, 1024], F32))

            def xt_tr(t, i, b):
                for c in range(DCH):
                    ins = t.transpose(psT[:, b, c * 128:(c + 1) * 128],
                                      XT3[:, c, i * 128:(i + 1) * 128], env["idf"][:, :])
                return ins

            gd_done = p.cnt.get("gd", 0)

            def xt_ev(v, i, b):
                if i == 0:
                    v.wait_ge(p.S("gd"), gd_done)
                return v.tensor_copy(xtok(i), psT[:, b, :])

            pipe(p, "l1fxt", TC, [
                ("tensor", xt_tr, 1),
                ("vector", xt_ev, 1),
            ], nbuf=2)
        if DEBUG_ATTN:
            with nc.Block() as blk:
                @blk.sync
                def _(e):
                    for tc in range(TC):
                        p.dma(e, dbg_xtok.ap()[tc * 128:(tc + 1) * 128, :], xtok(tc))
                    p.wait(e, "sd")
        emit_park_x(p, env)

        XQ = bv3(C, 0, TC, DM)
        SCRA = fv(SM, 0, DM)
        SCRD = fv(SM, 1024, DM)
        b_q = p.base("l1fq")
        b_a = p.base("l1fa")
        with nc.Block() as blk:
            @blk.vector
            def _(v):
                for tc in range(TC):
                    xt_ = xtok(tc)
                    if tc == 0:
                        p.wait(v, "l1fwsb")
                    p.inc(v.reduce_sum(st[:, 20:21], xt_, axis=AX.X), "l1fch", 1)
                    v.wait_ge(p.S("l1fch"), p.cnt["l1fch"])
                    p.inc(v.tensor_scalar(st[:, 20:21], st[:, 20:21], 1.0 / DM, None,
                                          op0=ALU.mult), "l1fq", 1)
                    v.wait_ge(p.S("l1fq"), b_q + 7 * tc + 1)
                    p.inc(v.tensor_scalar(xt_, xt_, st[:, 20:21], None,
                                          op0=ALU.subtract), "l1fq", 1)
                    v.wait_ge(p.S("l1fa"), b_a + 2 * tc + 2)
                    p.inc(v.reciprocal(st[:, 22:23], st[:, 22:23]), "l1fq", 1)
                    v.wait_ge(p.S("l1fq"), b_q + 7 * tc + 3)
                    p.inc(v.tensor_scalar(xt_, xt_, st[:, 22:23], None,
                                          op0=ALU.mult), "l1fq", 1)
                    p.inc(v.reduce_max(st[:, 23:24], xt_, axis=AX.X,
                                       apply_absolute_value=True), "l1fch", 1)
                    v.wait_ge(p.S("l1fch"), p.cnt["l1fch"])
                    p.inc(v.tensor_scalar(st[:, 23:24], st[:, 23:24], 1e-5, None,
                                          op0=ALU.max), "l1fq", 1)
                    v.wait_ge(p.S("l1fq"), b_q + 7 * tc + 5)
                    p.inc(v.reciprocal(st[:, 24:25], st[:, 23:24]), "l1fq", 1)
                    v.tensor_scalar(st[:, tc:tc + 1], st[:, 23:24], wsb[:, 1:2],
                                    1.0 / 127.0, op0=ALU.mult, op1=ALU.mult)
                    v.wait_ge(p.S("l1fq"), b_q + 7 * tc + 6)
                    p.inc(v.tensor_scalar(st[:, 24:25], st[:, 24:25], 127.0, None,
                                          op0=ALU.mult), "l1fq", 1)
                    v.wait_ge(p.S("l1fq"), b_q + 7 * tc + 7)
                    v.tensor_scalar(SCRD[:, :], xt_, st[:, 24:25], MAGIC,
                                    op0=ALU.mult, op1=ALU.add)
                    v.tensor_scalar(SCRD[:, :], SCRD[:, :], MAGIC, -128.0,
                                    op0=ALU.subtract, op1=ALU.max)
                    p.inc(v.tensor_scalar(XQ[:, tc, :], SCRD[:, :], 127.0, None,
                                          op0=ALU.min), "l1fqd", 1)
            @blk.scalar
            def _(a):
                for tc in range(TC):
                    a.wait_ge(p.S("l1fq"), b_q + 7 * tc + 2)
                    p.inc(a.activation(SCRA[:, :], xtok(tc), AF.Square,
                                       accum_out=st[:, 21:22]), "l1fa", 1)
                    p.inc(a.activation(st[:, 22:23], st[:, 21:22], AF.Sqrt,
                                       bias=env["cst"][:, C_LNEPS:C_LNEPS + 1],
                                       scale=env["cst"][:, C_INV1024:C_INV1024 + 1]),
                          "l1fa", 1)

        if DEBUG_ATTN:
            with nc.Block() as blk:
                @blk.sync
                def _(e):
                    p.dma(e, dbg_xq.ap().rearrange("(t p) n -> p t n", p=128), XQ[:, :, :])
                    p.wait(e, "sd")

        if DEBUG_ATTN:
            with nc.Block() as blk:
                @blk.sync
                def _(e):
                    p.dma(e, dbg_st.ap()[:, 0:26], st[:, :])
                    p.dma(e, dbg_st.ap()[:, 32:72], asum[:, :])
                    p.dma(e, dbg_st.ap()[:, 80:84], wsb[:, :])
                    p.dma(e, dbg_st.ap()[0:1, 84:128], rowb[:, :])
                    p.wait(e, "sd")

        XQT = bv3(A, 0, DCH, TL)
        with ExitStack() as es2:
            pstq = es2.enter_context(nc.psum_tensor("l1f_pstq", [128, 2, 1024], BF16))

            def q_tr(t, i, b):
                for c in range(DCH):
                    ins = t.transpose(pstq[:, b, c * 128:(c + 1) * 128],
                                      XQ[:, i, c * 128:(c + 1) * 128], env["idb"][:, :])
                return ins

            pipe(p, "l1fqt", TC, [
                ("tensor", q_tr, 1),
                ("vector", lambda v, i, b: v.tensor_copy(
                    XQT[:, :, i * 128:(i + 1) * 128],
                    pstq[:, b, :].rearrange("p (c n) -> p c n", c=DCH)), 1),
            ], nbuf=2)

        HS = fv3(XT, 0, 4, 4096)
        XQ2 = bv3(B, 0, 4, 4096)
        XQ2T = bv3(C, 0, 32, 512)
        W1QB = bv3(D, 0, 2, DCH * 512)
        W2QB = bv3(D, 0, 2, 32 * 128)
        B1R = None
        B1B = fv3(SM, 1024, 2, 512)
        UST = fv3(SM, 0, 2, 512)
        S2BC = fv(SM, 1024, 512)
        XPR = fv3(A, 4096, 2, 512)
        XPR = fv3(B, 4096, 2, 512)
        B1R = env["mbc"][0:1, 0:1024].rearrange("p (b n) -> p b n", b=2)

        for tsup in range(4):
            with ExitStack() as es2:
                psm = es2.enter_context(nc.psum_tensor(f"l1f_psm{tsup}", [128, 2, 512], F32))
                psb1 = es2.enter_context(nc.psum_tensor(f"l1f_psb1{tsup}", [128, 2, 512], F32))

                def m1_mm2(t, i, b):
                    hb, tt = divmod(i, 4)
                    if tt == 0:
                        t.matmul(psb1[:, hb % 2, :], env["ones1"][:, :], B1R[0:1, hb % 2, :],
                                 start=True, stop=True)
                    w3 = W1QB[:, hb % 2, :].rearrange("p (c n) -> p c n", c=DCH)
                    for c in range(DCH):
                        ins = t.matmul(psm[:, i % 2, :],
                                       XQT[:, c, tsup * 512 + tt * 128:
                                           tsup * 512 + (tt + 1) * 128],
                                       w3[:, c, :], start=(c == 0), stop=(c == DCH - 1))
                    return ins

                def m1_ld2(e, i, b):
                    hb, tt = divmod(i, 4)
                    p.inc(e.dma_start(out=B1R[0:1, hb % 2, :],
                                      in_=b1.ap()[:, hb * 512:(hb + 1) * 512]),
                          "l1fm1l", 16)
                    p.inc(e.dma_start(
                        out=W1QB[:, hb % 2, :].rearrange("p (c n) -> p c n", c=DCH),
                        in_=w1q.ap()[:, hb * 512:(hb + 1) * 512]
                            .rearrange("(c p) n -> p c n", p=128)), "l1fm1l", 16)

                def m1_ep(v, i, b):
                    hb, tt = divmod(i, 4)
                    if tt == 0:
                        v.tensor_copy(B1B[:, hb % 2, :], psb1[:, hb % 2, :])
                    return v.scalar_tensor_tensor(
                        UST[:, i % 2, :], psm[:, i % 2, :],
                        st[:, tsup * 4 + tt:tsup * 4 + tt + 1], B1B[:, hb % 2, :],
                        op0=ALU.mult, op1=ALU.add)

                def m1_gel(a, i, b):
                    hb, tt = divmod(i, 4)
                    return a.activation(HS[:, tt, hb * 512:(hb + 1) * 512],
                                        UST[:, i % 2, :], AF.Gelu)

                b_ld = p.base("l1fm1l"); b_mm = p.base("l1fm1m")
                b_ep = p.base("l1fm1e"); b_ge = p.base("l1fm1g")
                with nc.Block() as blk:
                    @blk.sync
                    def _(e):
                        for hb in range(8):
                            if hb >= 2:
                                e.wait_ge(p.S("l1fm1m"), b_mm + (hb - 1) * 4)
                            m1_ld2(e, hb * 4, None)
                    @blk.tensor
                    def _(t):
                        for i in range(32):
                            hb, tt = divmod(i, 4)
                            if tt == 0:
                                t.wait_ge(p.S("l1fm1l"), b_ld + (hb + 1) * 32)
                            if i >= 2:
                                t.wait_ge(p.S("l1fm1e"), b_ep + i - 1)
                            p.inc(m1_mm2(t, i, None), "l1fm1m", 1)
                    @blk.vector
                    def _(v):
                        for i in range(32):
                            v.wait_ge(p.S("l1fm1m"), b_mm + i + 1)
                            if i >= 2:
                                v.wait_ge(p.S("l1fm1g"), b_ge + i - 1)
                            p.inc(m1_ep(v, i, None), "l1fm1e", 1)
                    @blk.scalar
                    def _(a):
                        for i in range(32):
                            a.wait_ge(p.S("l1fm1e"), b_ep + i + 1)
                            p.inc(m1_gel(a, i, None), "l1fm1g", 1)

            if DEBUG_ATTN and tsup == 0:
                with nc.Block() as blk:
                    @blk.sync
                    def _(e):
                        p.dma(e, dbg_hs.ap().rearrange("(t p) n -> p t n", p=128),
                              HS[:, :, :])
                        p.wait(e, "sd")

            b_q2 = p.base("l1fq2")
            b_a2 = p.base("l1fa2")
            with nc.Block() as blk:
                @blk.vector
                def _(v):
                    for tt in range(4):
                        h_ = HS[:, tt, :]
                        p.inc(v.reduce_max(st[:, 23:24], h_, axis=AX.X,
                                           apply_absolute_value=True), "l1fch", 1)
                        v.wait_ge(p.S("l1fch"), p.cnt["l1fch"])
                        p.inc(v.tensor_scalar(st[:, 23:24], st[:, 23:24], 1e-5, None,
                                              op0=ALU.max), "l1fq2", 1)
                        v.wait_ge(p.S("l1fq2"), b_q2 + 3 * tt + 1)
                        p.inc(v.reciprocal(st[:, 24:25], st[:, 23:24]), "l1fq2", 1)
                        v.tensor_scalar(st[:, 16 + tt:17 + tt], st[:, 23:24], wsb[:, 3:4],
                                        1.0 / 127.0, op0=ALU.mult, op1=ALU.mult)
                        v.wait_ge(p.S("l1fq2"), b_q2 + 3 * tt + 2)
                        p.inc(v.tensor_scalar(st[:, 24:25], st[:, 24:25], 127.0, None,
                                              op0=ALU.mult), "l1fq2", 1)
                        v.wait_ge(p.S("l1fq2"), b_q2 + 3 * tt + 3)
                        for qq in range(0, 4):
                            v.tensor_scalar(SCRD[:, 0:1024], h_[:, qq * 1024:(qq + 1) * 1024],
                                            st[:, 24:25], MAGIC, op0=ALU.mult, op1=ALU.add)
                            v.tensor_scalar(SCRD[:, 0:1024], SCRD[:, 0:1024], MAGIC, -128.0,
                                            op0=ALU.subtract, op1=ALU.max)
                            p.inc(v.tensor_scalar(XQ2[:, tt, qq * 1024:(qq + 1) * 1024],
                                                  SCRD[:, 0:1024], 127.0, None,
                                                  op0=ALU.min), "l1fq2d", 1)

            with ExitStack() as es2:
                pst2 = es2.enter_context(nc.psum_tensor(f"l1f_pst2{tsup}", [128, 2, 1024],
                                                        BF16))

                def q2_tr(t, i, b):
                    tt, g = divmod(i, 4)
                    for k in range(8):
                        hc = g * 8 + k
                        ins = t.transpose(pst2[:, b, k * 128:(k + 1) * 128],
                                          XQ2[:, tt, hc * 128:(hc + 1) * 128],
                                          env["idb"][:, :])
                    return ins

                def q2_ev(v, i, b):
                    tt, g = divmod(i, 4)
                    return v.tensor_copy(
                        XQ2T[:, g * 8:(g + 1) * 8, tt * 128:(tt + 1) * 128],
                        pst2[:, b, :].rearrange("p (k n) -> p k n", k=8))

                pipe(p, f"l1fq2t{tsup}", 16, [("tensor", q2_tr, 1),
                                              ("vector", q2_ev, 1)], nbuf=2)

            with ExitStack() as es2:
                pss2 = es2.enter_context(nc.psum_tensor(f"l1f_pss2{tsup}", [4, 512], F32))
                psbb = es2.enter_context(nc.psum_tensor(f"l1f_psbb{tsup}", [128, 2, 512],
                                                        F32))
                s2r = env["mbc"][0:4, 1664:1792]
                with nc.Block() as blk:
                    @blk.tensor
                    def _(t):
                        p.inc(t.transpose(pss2[:, 0:128], st[:, 16:20], env["idf"][:, :]),
                              "l1fs2a", 1)
                    @blk.vector
                    def _(v):
                        p.wait(v, "l1fs2a")
                        p.inc(v.tensor_copy(s2r[:, :], pss2[:, 0:128]), "l1fs2b", 1)
                b_c = p.base("l1fs2c"); b_d = p.base("l1fs2d")
                with nc.Block() as blk:
                    @blk.tensor
                    def _(t):
                        p.wait(t, "l1fs2b")
                        for tt in range(4):
                            if tt >= 2:
                                t.wait_ge(p.S("l1fs2d"), b_d + tt - 1)
                            p.inc(t.matmul(psbb[:, tt % 2, 0:128],
                                           e4s[:, tt * 128:(tt + 1) * 128],
                                           s2r[:, :], start=True, stop=True), "l1fs2c", 1)
                    @blk.vector
                    def _(v):
                        for tt in range(4):
                            v.wait_ge(p.S("l1fs2c"), b_c + tt + 1)
                            p.inc(v.tensor_copy(S2BC[:, tt * 128:(tt + 1) * 128],
                                                psbb[:, tt % 2, 0:128]), "l1fs2d", 1)

            with ExitStack() as es2:
                psm2 = es2.enter_context(nc.psum_tensor(f"l1f_psm2{tsup}", [128, 2, 512],
                                                        F32))
                bxp = p.base(f"l1fxp{tsup}")

                def m2_ld(e, i, b):
                    p.inc(e.dma_start(out=XPR[:, b, :],
                                      in_=env["x_park"].ap()[i * 128:(i + 1) * 128,
                                                             tsup * 512:(tsup + 1) * 512]),
                          f"l1fxp{tsup}", 16)
                    return e.dma_start(
                        out=W2QB[:, b, :].rearrange("p (k n) -> p k n", k=32),
                        in_=w2q.ap()[:, i * 128:(i + 1) * 128]
                            .rearrange("(k p) n -> p k n", p=128))

                def m2_mm(t, i, b):
                    w3 = W2QB[:, b, :].rearrange("p (k n) -> p k n", k=32)
                    for hc in range(32):
                        ins = t.matmul(psm2[:, b, :], w3[:, hc, :],
                                       XQ2T[:, hc, :], start=(hc == 0), stop=(hc == 31))
                    return ins

                def m2_ep(v, i, b):
                    v.wait_ge(p.S(f"l1fxp{tsup}"), bxp + (i + 1) * 16)
                    v.tensor_tensor(UST[:, b, :], psm2[:, b, :], S2BC[:, :], op=ALU.mult)
                    return v.scalar_tensor_tensor(
                        UST[:, b, :], UST[:, b, :],
                        b2_pc[:, i:i + 1], XPR[:, b, :], op0=ALU.add, op1=ALU.add)

                def m2_st(e, i, b):
                    return e.dma_start(
                        out=env["x_park"].ap()[i * 128:(i + 1) * 128,
                                               tsup * 512:(tsup + 1) * 512],
                        in_=UST[:, b, :])

                pipe(p, f"l1fm2{tsup}", DCH, [
                    ("sync", m2_ld, 16),
                    ("tensor", m2_mm, 1),
                    ("vector", m2_ep, 1),
                    ("gpsimd", m2_st, 16),
                ], nbuf=2)
        emit_unpark_x(p, env)


EMIT["l1f"] = emit_l1_ffn


def emit_l0_ffn(p, env):
    nc = p.nc
    XT, A, B, C, D, SM = env["XT"], env["A"], env["B"], env["C"], env["D"], env["SM"]
    XT3 = xt3(XT)
    wq = nc.dram_tensor("l0f_wq", [DM, 512], F32, kind="ExternalInput")
    wk = nc.dram_tensor("l0f_wk", [DM, 512], F32, kind="ExternalInput")
    wv = nc.dram_tensor("l0f_wv", [DM, 256], F32, kind="ExternalInput")
    wo = nc.dram_tensor("l0f_wo", [512, DM], F32, kind="ExternalInput")
    lnw = nc.dram_tensor("l0f_lnw", [128, 16], F32, kind="ExternalInput")
    rwt = nc.dram_tensor("l0f_rw", [DM, 8], F32, kind="ExternalInput")
    wg = nc.dram_tensor("l0f_wg", [8, DM, 3584], F32, kind="ExternalInput")
    wu = nc.dram_tensor("l0f_wu", [8, DM, 3584], F32, kind="ExternalInput")
    wd = nc.dram_tensor("l0f_wd", [8, 3584, DM], F32, kind="ExternalInput")
    hag_l = nc.dram_tensor("l0f_hagl", [DM, TL], BF16, kind="Internal")

    with ExitStack() as es:
        lnpc = es.enter_context(nc.sbuf_tensor("l0f_lnpc", [128, 16], F32))
        with nc.Block() as blk:
            @blk.sync
            def _(e):
                p.dma(e, lnpc[:, :], lnw[:, :])
                p.wait(e, "sd")
        ln1_pc = lnpc[:, 0:8]
        ln2_pc = lnpc[:, 8:16]

        emit_ln_full(p, env, XT3, rms=False, eps_col=C_LNEPS, tag="l0fln1", W=B)

        SRC = fv3(SM, 0, 2, 512)
        HST = bv3(SM, 1024, 2, 512)

        def h1_tile(v, k, b):
            tb, cd = divmod(k, DCH)
            sl = slice(tb * 512, (tb + 1) * 512)
            v.tensor_tensor(SRC[:, b, :], XT3[:, cd, sl], env["mbc"][:, sl],
                            op=ALU.subtract)
            return v.tensor_tensor(SRC[:, b, :], SRC[:, b, :], env["rbc"][:, sl],
                                   op=ALU.mult)

        with ExitStack() as es2:
            sqs2 = B[0:1, 2048:2048 + TL]
            var2 = B[0:1, 4096:4096 + TL]
            sqb2 = fv3(B, 6144, 2, 512)
            psq2 = es2.enter_context(nc.psum_tensor("l0f_psq2", [1, 512], F32))
            b_h = p.base("l0fh1"); b_q = p.base("l0frq")
            b_m = p.base("l0frm"); b_e = p.base("l0fre")
            with nc.Block() as blk:
                @blk.vector
                def _(v):
                    for k in range(TB * DCH):
                        tb, cd = divmod(k, DCH)
                        if k >= 2:
                            v.wait_ge(p.S("l0frq"), b_q + k - 1)
                        p.inc(h1_tile(v, k, k % 2), "l0fh1", 1)
                        if cd == DCH - 1:
                            v.wait_ge(p.S("l0frm"), b_m + (tb + 1) * DCH)
                            p.inc(v.tensor_copy(sqs2[:, tb * 512:(tb + 1) * 512],
                                                psq2[:, :]), "l0fre", 1)
                @blk.scalar
                def _(a):
                    for k in range(TB * DCH):
                        a.wait_ge(p.S("l0fh1"), b_h + k + 1)
                        if k >= 2:
                            a.wait_ge(p.S("l0frm"), b_m + k - 1)
                        p.inc(a.square(sqb2[:, k % 2, :], SRC[:, k % 2, :]), "l0frq", 1)
                @blk.tensor
                def _(t):
                    for k in range(TB * DCH):
                        tb, cd = divmod(k, DCH)
                        if cd == 0 and tb > 0:
                            t.wait_ge(p.S("l0fre"), b_e + tb)
                        t.wait_ge(p.S("l0frq"), b_q + k + 1)
                        p.inc(t.matmul(psq2[:, :], env["onesc"][:, :], sqb2[:, k % 2, :],
                                       start=(cd == 0), stop=(cd == DCH - 1)), "l0frm", 1)
            with nc.Block() as blk:
                @blk.vector
                def _(v):
                    p.inc(v.tensor_scalar(var2[:, :], sqs2[:, :], 1.0 / DM, None,
                                          op0=ALU.mult), "l0fvc", 1)
                @blk.scalar
                def _(a):
                    p.wait(a, "l0fvc")
                    p.inc(a.activation(var2[:, :], var2[:, :], AF.Sqrt,
                                       bias=env["cst"][0:1, C_RMSEPS:C_RMSEPS + 1],
                                       scale=1.0), "l0fvs", 1)
            with nc.Block() as blk:
                @blk.vector
                def _(v):
                    p.wait(v, "l0fvs")
                    v.reciprocal(var2[:, :], var2[:, :])
                @blk.scalar
                def _(a):
                    pass
            RBC2 = fv(B, 0, TL)
            emit_bc(p, env, var2, RBC2, "l0frb")
            b_p = p.base("l0fhp")

            def hr_ops(v, k, b):
                tb, cd = divmod(k, DCH)
                sl = slice(tb * 512, (tb + 1) * 512)
                h1_tile(v, k, b)
                v.tensor_tensor(SRC[:, b, :], SRC[:, b, :], RBC2[:, sl], op=ALU.mult)
                return v.tensor_scalar(HST[:, b, :], SRC[:, b, :], ln1_pc[:, cd:cd + 1],
                                       None, op0=ALU.mult)

            def hr_st(e, k, b):
                tb, cd = divmod(k, DCH)
                return e.dma_start(out=hag_l.ap()[cd * 128:(cd + 1) * 128,
                                                  tb * 512:(tb + 1) * 512],
                                   in_=HST[:, b, :])

            pipe(p, "l0fhr", TB * DCH, [("vector", hr_ops, 1), ("sync", hr_st, 16)],
                 nbuf=2)

        emit_park_x(p, env)
        cfg = dict(pfx="l0f", kch=4, vch=2, kv_shared=True, wq=wq, wk=wk, wv=wv, wo=wo,
                   bq=None, bk=None, bv=None, hag_l=hag_l)
        om, OMR = emit_causal_attn(p, env, cfg)
        emit_unpark_x(p, env)

        with ExitStack() as es2:
            psT = es2.enter_context(nc.psum_tensor("l0f_psT", [128, 2, 1024], F32))
            SC1 = fv3(SM, 0, 2, 128)
            SC2 = fv3(SM, 256, 2, 128)
            XRS = fv3(D, 0, 2, 1024)

            def x1_ld(e, i, b):
                return e.dma_start(out=OMR[:, b, :],
                                   in_=om.ap()[i * 128:(i + 1) * 128, :])

            def x1_tr(t, i, b):
                for c in range(DCH):
                    ins = t.transpose(psT[:, b, c * 128:(c + 1) * 128],
                                      OMR[:, b, c * 128:(c + 1) * 128], env["idf"][:, :])
                return ins

            def x1_ops(v, i, b):
                sl = slice(i * 128, (i + 1) * 128)
                for c in range(DCH):
                    v.tensor_tensor(SC1[:, b, :], XT3[:, c, sl], env["mbc"][:, sl],
                                    op=ALU.subtract)
                    v.tensor_tensor(SC1[:, b, :], SC1[:, b, :], env["rbc"][:, sl],
                                    op=ALU.mult)
                    v.tensor_tensor(SC1[:, b, :], SC1[:, b, :],
                                    psT[:, b, c * 128:(c + 1) * 128], op=ALU.add)
                    v.tensor_tensor(XRS[:, b, c * 128:(c + 1) * 128], XT3[:, c, sl],
                                    SC1[:, b, :], op=ALU.add)
                    ins = v.tensor_copy(XT3[:, c, sl], SC1[:, b, :])
                return ins

            def x1_st(e, i, b):
                return e.dma_start(
                    out=env["x_park"].ap().rearrange("(c p) n -> p c n", p=128)
                        [:, :, i * 128:(i + 1) * 128],
                    in_=XRS[:, b, :].rearrange("p (c n) -> p c n", c=DCH))

            pipe(p, "l0fx1", TC, [
                ("sync", x1_ld, 16),
                ("tensor", x1_tr, 1),
                ("vector", x1_ops, 1),
                ("gpsimd", x1_st, 16),
            ], nbuf=2)
        with nc.Block() as blk:
            @blk.sync
            def _(e):
                p.wait(e, "l0fx13")

        if DEBUG_ATTN:
            dbg_x1 = nc.dram_tensor("l0f_dbgx1", [DM, TL], F32, kind="ExternalOutput")
            with nc.Block() as blk:
                @blk.sync
                def _(e):
                    p.dma(e, dbg_x1.ap().rearrange("(c p) n -> p c n", p=128), XT3[:, :, :])
                    p.wait(e, "sd")

        emit_ln_full(p, env, XT3, rms=True, eps_col=C_RMSEPS, tag="l0fln2", W=B)
        HT = bv3(A, 0, DCH, TL)
        emit_apply_ln(p, env, XT3, HT, tmp=fv(SM, 0, 512), gain_pc=ln2_pc,
                      use_mean=False)

        WT8 = env["rbc"][0:8, :]
        with ExitStack() as es2:
            RB = es2.enter_context(nc.sbuf_tensor("l0f_rb", [128, 8, 8], BF16))
            LG = es2.enter_context(nc.sbuf_tensor("l0f_lg", [128, 2, 8], F32))
            MSK = es2.enter_context(nc.sbuf_tensor("l0f_msk", [128, 8], F32))
            TM = es2.enter_context(nc.sbuf_tensor("l0f_tm", [128, 8], F32))
            EX = es2.enter_context(nc.sbuf_tensor("l0f_ex", [128, 2, 8], F32))
            SEL = es2.enter_context(nc.sbuf_tensor("l0f_sel", [128, 8], F32))
            W8 = es2.enter_context(nc.sbuf_tensor("l0f_w8", [128, 2, 8], F32))
            rst = es2.enter_context(nc.sbuf_tensor("l0f_rst", [128, 8], F32))
            psr = es2.enter_context(nc.psum_tensor("l0f_psr", [128, 2, 512], F32))
            psw = es2.enter_context(nc.psum_tensor("l0f_psw", [8, 2, 512], F32))
            emit_w_load(p, RB, rwt, nchunk=DCH, cols=8)
            b_rm = p.base("l0fram"); b_r5 = p.base("l0frv5"); b_rv = p.base("l0frav")
            b_ra = p.base("l0fraa"); b_rt = p.base("l0frat"); b_rw = p.base("l0fraw")
            with nc.Block() as blk:
                @blk.tensor
                def _(t):
                    for tc in range(TC):
                        if tc >= 2:
                            t.wait_ge(p.S("l0frav"), b_rv + tc - 1)
                        for c in range(DCH):
                            ins = t.matmul(psr[:, tc % 2, 0:8],
                                           HT[:, c, tc * 128:(tc + 1) * 128],
                                           RB[:, c, :], start=(c == 0),
                                           stop=(c == DCH - 1))
                        p.inc(ins, "l0fram", 1)
                        if tc >= 1:
                            t.wait_ge(p.S("l0frav"), b_rv + tc)
                            if tc >= 2:
                                t.wait_ge(p.S("l0fraw"), b_rw + tc - 1)
                            p.inc(t.transpose(psw[:, (tc - 1) % 2, 0:128],
                                              W8[:, (tc - 1) % 2, :], env["idf"][:, :]),
                                  "l0frat", 1)
                    t.wait_ge(p.S("l0frav"), b_rv + TC)
                    p.inc(t.transpose(psw[:, (TC - 1) % 2, 0:128],
                                      W8[:, (TC - 1) % 2, :], env["idf"][:, :]),
                          "l0frat", 1)
                @blk.vector
                def _(v):
                    def edge(ins):
                        p.inc(ins, "l0fre2", 1)
                        v.wait_ge(p.S("l0fre2"), p.cnt["l0fre2"])
                    for tc in range(TC):
                        bb = tc % 2
                        v.wait_ge(p.S("l0fram"), b_rm + tc + 1)
                        edge(v.reduce_max(rst[:, 0:1], psr[:, bb, 0:8], axis=AX.X,
                                          negate=True))
                        edge(v.tensor_scalar(LG[:, bb, :], psr[:, bb, 0:8], rst[:, 0:1],
                                             None, op0=ALU.add))
                        edge(v.tensor_scalar(MSK[:, :], LG[:, bb, :], 0.0, None,
                                             op0=ALU.is_ge))
                        edge(v.scalar_tensor_tensor(TM[:, :], MSK[:, :], -3e38,
                                                    LG[:, bb, :], op0=ALU.mult,
                                                    op1=ALU.add))
                        ins = v.reduce_max(rst[:, 1:2], TM[:, :], axis=AX.X)
                        p.inc(ins, "l0frv5", 1)
                        v.wait_ge(p.S("l0frv5"), p.cnt["l0frv5"])
                        v.wait_ge(p.S("l0fraa"), b_ra + tc + 1)
                        edge(v.tensor_scalar(rst[:, 3:4], rst[:, 2:3], 1.0, None,
                                             op0=ALU.add))
                        edge(v.reciprocal(rst[:, 4:5], rst[:, 3:4]))
                        edge(v.tensor_scalar(SEL[:, :], LG[:, bb, :], rst[:, 1:2],
                                             None, op0=ALU.is_ge))
                        v.tensor_tensor(W8[:, bb, :], EX[:, bb, :], SEL[:, :],
                                        op=ALU.mult)
                        p.inc(v.tensor_scalar(W8[:, bb, :], W8[:, bb, :], rst[:, 4:5],
                                              None, op0=ALU.mult), "l0frav", 1)
                        if tc >= 1:
                            v.wait_ge(p.S("l0frat"), b_rt + tc)
                            p.inc(v.tensor_copy(WT8[:, (tc - 1) * 128:tc * 128],
                                                psw[:, (tc - 1) % 2, 0:128]),
                                  "l0fraw", 1)
                    v.wait_ge(p.S("l0frat"), b_rt + TC)
                    p.inc(v.tensor_copy(WT8[:, (TC - 1) * 128:TC * 128],
                                        psw[:, (TC - 1) % 2, 0:128]), "l0fraw", 1)
                @blk.scalar
                def _(a):
                    for tc in range(TC):
                        bb = tc % 2
                        a.wait_ge(p.S("l0frv5"), b_r5 + tc + 1)
                        a.activation(EX[:, bb, :], LG[:, bb, :], AF.Exp)
                        p.inc(a.activation(rst[:, 2:3], rst[:, 1:2], AF.Exp), "l0fraa", 1)

        if DEBUG_ATTN:
            dbg_w8 = nc.dram_tensor("l0f_dbgw8", [8, TL], F32, kind="ExternalOutput")
            with nc.Block() as blk:
                @blk.sync
                def _(e):
                    p.dma(e, dbg_w8.ap(), WT8)
                    p.wait(e, "sd")

        GM = 14
        WBC = bv(XT, 14336, 4096)[:, 0:TL * 2]
        WBC = bv(XT, 14336, TL)
        ACT3 = bv3(XT, 0, GM, TL)
        WG3 = bv3(D, 0, 2, DCH * 128)
        WU3 = bv3(D, 1024, 2, DCH * 128)
        WDh = bv3(D, 0, GM, 512)
        SGT = fv3(SM, 0, 2, 512)
        UMT = fv3(SM, 1024, 2, 512)

        def moe_acc(dc, tb):
            ar = B if dc < 4 else C
            return fv(ar, (dc % 4) * TL + tb * 512, 512)

        e8c = env["cdr"]["e8"]
        E8S = es.enter_context(nc.sbuf_tensor("l0f_e8s", [8, DM], F32))
        with nc.Block() as blk:
            @blk.sync
            def _(e):
                p.dma(e, E8S[:, :], e8c[:, :])
                p.wait(e, "sd")

        for ex in range(8):
            with ExitStack() as es2:
                psb = es2.enter_context(nc.psum_tensor(f"l0f_pwb{ex}", [128, 2, 512], F32))
                pipe(p, f"l0fwb{ex}", TB, [
                    ("tensor", lambda t, i, b: t.matmul(
                        psb[:, b, :], E8S[:, ex * 128:(ex + 1) * 128],
                        WT8[:, i * 512:(i + 1) * 512], start=True, stop=True), 1),
                    ("vector", lambda v, i, b: v.tensor_copy(
                        WBC[:, i * 512:(i + 1) * 512], psb[:, b, :]), 1),
                ], nbuf=2)
            for g in range(2):
                m0 = g * GM
                with ExitStack() as es2:
                    psg = es2.enter_context(nc.psum_tensor(f"l0f_psg{ex}_{g}",
                                                           [128, 2, 512], F32))
                    psu = es2.enter_context(nc.psum_tensor(f"l0f_psu{ex}_{g}",
                                                           [128, 2, 512], F32))
                    b_gd = p.base("gd")
                    b_mg = p.base("l0fmg"); b_ms = p.base("l0fms")
                    b_mu = p.base("l0fmu"); b_ma = p.base("l0fma")
                    with nc.Block() as blk:
                        @blk.gpsimd
                        def _(gp):
                            for m in (0, 1):
                                p.dma(gp, WG3[:, m % 2, :].rearrange(
                                    "p (c n) -> p c n", c=DCH),
                                    wg.ap()[ex, :, (m0 + m) * 128:(m0 + m + 1) * 128]
                                    .rearrange("(c p) n -> p c n", p=128), q="gd")
                                p.dma(gp, WU3[:, m % 2, :].rearrange(
                                    "p (c n) -> p c n", c=DCH),
                                    wu.ap()[ex, :, (m0 + m) * 128:(m0 + m + 1) * 128]
                                    .rearrange("(c p) n -> p c n", p=128), q="gd")
                            for m in range(GM):
                                for tb in range(TB):
                                    i = m * TB + tb
                                    gp.wait_ge(p.S("l0fms"), b_ms + i + 1)
                                    gp.wait_ge(p.S("l0fmu"), b_mu + i + 1)
                                    p.inc(gp.tensor_tensor(
                                        ACT3[:, m, tb * 512:(tb + 1) * 512],
                                        SGT[:, i % 2, :], UMT[:, i % 2, :],
                                        op=ALU.mult), "l0fma", 1)
                                if m + 2 < GM:
                                    gp.wait_ge(p.S("l0fmg"), b_mg + (m + 1) * TB)
                                    p.dma(gp, WG3[:, m % 2, :].rearrange(
                                        "p (c n) -> p c n", c=DCH),
                                        wg.ap()[ex, :, (m0 + m + 2) * 128:
                                                (m0 + m + 3) * 128]
                                        .rearrange("(c p) n -> p c n", p=128), q="gd")
                                    p.dma(gp, WU3[:, m % 2, :].rearrange(
                                        "p (c n) -> p c n", c=DCH),
                                        wu.ap()[ex, :, (m0 + m + 2) * 128:
                                                (m0 + m + 3) * 128]
                                        .rearrange("(c p) n -> p c n", p=128), q="gd")
                        @blk.tensor
                        def _(t):
                            for m in range(GM):
                                t.wait_ge(p.S("gd"), b_gd + (m + 1) * 32)
                                wgv = WG3[:, m % 2, :].rearrange("p (c n) -> p c n", c=DCH)
                                wuv = WU3[:, m % 2, :].rearrange("p (c n) -> p c n", c=DCH)
                                for tb in range(TB):
                                    i = m * TB + tb
                                    if i >= 2:
                                        t.wait_ge(p.S("l0fms"), b_ms + i - 1)
                                        t.wait_ge(p.S("l0fmu"), b_mu + i - 1)
                                    sl = slice(tb * 512, (tb + 1) * 512)
                                    for c in range(DCH):
                                        t.matmul(psg[:, i % 2, :], wgv[:, c, :],
                                                 HT[:, c, sl], start=(c == 0),
                                                 stop=(c == DCH - 1))
                                    for c in range(DCH):
                                        ins = t.matmul(psu[:, i % 2, :], wuv[:, c, :],
                                                       HT[:, c, sl], start=(c == 0),
                                                       stop=(c == DCH - 1))
                                    p.inc(ins, "l0fmg", 1)
                        @blk.scalar
                        def _(a):
                            for i in range(GM * TB):
                                a.wait_ge(p.S("l0fmg"), b_mg + i + 1)
                                if i >= 2:
                                    a.wait_ge(p.S("l0fma"), b_ma + i - 1)
                                p.inc(a.activation(SGT[:, i % 2, :], psg[:, i % 2, :],
                                                   AF.Silu), "l0fms", 1)
                        @blk.vector
                        def _(v):
                            for i in range(GM * TB):
                                m, tb = divmod(i, TB)
                                v.wait_ge(p.S("l0fmg"), b_mg + i + 1)
                                if i >= 2:
                                    v.wait_ge(p.S("l0fma"), b_ma + i - 1)
                                p.inc(v.tensor_tensor(
                                    UMT[:, i % 2, :], psu[:, i % 2, :],
                                    WBC[:, tb * 512:(tb + 1) * 512], op=ALU.mult),
                                    "l0fmu", 1)
                if DEBUG_ATTN and ex == 0 and g == 0:
                    dbg_a3 = nc.dram_tensor("l0f_dbga3", [GM * 128, TL], BF16,
                                            kind="ExternalOutput")
                    dbg_wbc = nc.dram_tensor("l0f_dbgwbc", [128, TL], BF16,
                                             kind="ExternalOutput")
                    with nc.Block() as blk:
                        @blk.sync
                        def _(e):
                            p.dma(e, dbg_a3.ap().rearrange("(m p) n -> p m n", p=128),
                                  ACT3[:, :, :])
                            p.dma(e, dbg_wbc.ap(), WBC[:, :])
                            p.wait(e, "sd")

                for hh in range(2):
                    with ExitStack() as es2:
                        psd = es2.enter_context(nc.psum_tensor(
                            f"l0f_psd{ex}_{g}_{hh}", [128, 2, 512], F32))
                        b_gd2 = p.base("gd")
                        b_dm = p.base("l0fdm"); b_dv = p.base("l0fdv")
                        with nc.Block() as blk:
                            @blk.gpsimd
                            def _(gp):
                                p.wait(gp, "gd")
                                p.dma(gp, WDh[:, :, :].rearrange(
                                    "p m (d n) -> p m d n", d=4),
                                    wd.ap()[ex, m0 * 128:(m0 + GM) * 128,
                                            hh * 512:(hh + 1) * 512]
                                    .rearrange("(m p) (d n) -> p m d n", p=128, n=128),
                                    q="gd")
                            @blk.tensor
                            def _(t):
                                t.wait_ge(p.S("gd"), b_gd2 + 16)
                                wdv = WDh[:, :, :].rearrange("p m (d n) -> p m d n", d=4)
                                for j in range(16):
                                    tb, dc = divmod(j, 4)
                                    if j >= 2:
                                        t.wait_ge(p.S("l0fdv"), b_dv + j - 1)
                                    for m in range(GM):
                                        ins = t.matmul(
                                            psd[:, j % 2, :], wdv[:, m, dc, :],
                                            ACT3[:, m, tb * 512:(tb + 1) * 512],
                                            start=(m == 0), stop=(m == GM - 1))
                                    p.inc(ins, "l0fdm", 1)
                            @blk.vector
                            def _(v):
                                for j in range(16):
                                    tb, dc = divmod(j, 4)
                                    v.wait_ge(p.S("l0fdm"), b_dm + j + 1)
                                    dst = moe_acc(hh * 4 + dc, tb)
                                    if ex == 0 and g == 0:
                                        ins = v.tensor_copy(dst, psd[:, j % 2, :])
                                    else:
                                        ins = v.tensor_tensor(dst, dst, psd[:, j % 2, :],
                                                              op=ALU.add)
                                    p.inc(ins, "l0fdv", 1)

        if DEBUG_ATTN:
            dbg_moe = nc.dram_tensor("l0f_dbgmoe", [DM, TL], F32, kind="ExternalOutput")
            dbg_xp = nc.dram_tensor("l0f_dbgxp", [DM, TL], F32, kind="ExternalOutput")
            with nc.Block() as blk:
                @blk.sync
                def _(e):
                    for dc in range(DCH):
                        for tb in range(TB):
                            p.dma(e, dbg_moe.ap()[dc * 128:(dc + 1) * 128,
                                                  tb * 512:(tb + 1) * 512],
                                  moe_acc(dc, tb))
                    p.wait(e, "gd")
                    p.dma(e, dbg_xp.ap(), env["x_park"].ap())
                    p.wait(e, "sd")

        XPR2 = fv3(D, 0, 2, 512)

        gd_now = p.cnt.get("gd", 0)

        def f_ld(e, j, b):
            if j == 0:
                e.wait_ge(p.S("gd"), gd_now)
            dc, tb = divmod(j, TB)
            return e.dma_start(out=XPR2[:, b, :],
                               in_=env["x_park"].ap()[dc * 128:(dc + 1) * 128,
                                                      tb * 512:(tb + 1) * 512])

        def f_add(v, j, b):
            dc, tb = divmod(j, TB)
            return v.tensor_tensor(XT3[:, dc, tb * 512:(tb + 1) * 512],
                                   moe_acc(dc, tb), XPR2[:, b, :], op=ALU.add)

        pipe(p, "l0ffin", DCH * TB, [("sync", f_ld, 16), ("vector", f_add, 1)], nbuf=2)
        if DEBUG_ATTN:
            dbg_xf = nc.dram_tensor("l0f_dbgxf", [DM, TL], F32, kind="ExternalOutput")
            with nc.Block() as blk:
                emit_settle(p, env, blk)
                @blk.sync
                def _(e):
                    p.wait(e, "settle")
                    for dc in range(DCH):
                        p.dma(e, dbg_xf.ap()[dc * 128:(dc + 1) * 128, :],
                              xt3(env["XT"])[:, dc, :])
                    p.wait(e, "sd")


EMIT["l0f"] = emit_l0_ffn
